# version 02
# baseline (speedup 1.0000x reference)
import contextlib
import numpy as np
import ml_dtypes
import concourse.bass as bass
import concourse.mybir as mybir

F32 = mybir.dt.float32
BF16 = mybir.dt.bfloat16
AF = mybir.ActivationFunctionType
ALU = mybir.AluOpType
bf16 = ml_dtypes.bfloat16

DM = 2048
PROJ = 8512
ZD = 4096
XBC = 4352
NH = 64
HP = 64
NS = 128
CS = 256
KC = DM // 128
ZC = ZD // 128
XCC = XBC // 128
HID = 8192
EPS = 1e-5
WT = 512


class SemT:
    __slots__ = ("h", "n", "name")

    def __init__(self, es, nc, name):
        self.h = es.enter_context(nc.semaphore(name))
        self.n = 0
        self.name = name


class Prog:
    ENG = ("tensor", "vector", "scalar", "gpsimd", "sync")

    def __init__(self):
        self.q = {e: [] for e in self.ENG}
        self.mark = {e: {} for e in self.ENG}

    def wait(self, eng, sem, val=None):
        vv = sem.n if val is None else val
        if vv <= 0 or self.mark[eng].get(sem.name, 0) >= vv:
            return
        self.mark[eng][sem.name] = vv
        self.q[eng].append(("w", sem.h, vv))

    def op(self, eng, fn, incs=()):
        self.q[eng].append(("o", fn, list(incs)))
        for s, k in incs:
            s.n += k

    def emit(self, block):
        for name in self.ENG:
            fns = self.q[name]
            if not fns:
                continue

            def body(eng, fns=fns):
                for rec in fns:
                    if rec[0] == "w":
                        eng.wait_ge(rec[1], rec[2])
                    else:
                        ins = rec[1](eng)
                        for s, k in rec[2]:
                            ins.then_inc(s.h, k)
            getattr(block, name)(body)


class Builder:
    def __init__(self, T=2048, debug=False, grp=4):
        self.T = T
        self.NT = T // WT
        self.NCH = T // CS
        self.debug = debug
        self.grp = grp
        self.nc = bass.Bass("TRN2", target_bir_lowering=False)
        self.es = contextlib.ExitStack()
        self.sems = {}
        self.p = Prog()
        self._mk_dram()
        self._mk_sbuf()
        self.SV = self.S("sv")
        self.SA = self.S("sa")
        self.SP = self.S("sp")
        self.SG = self.S("sg")
        self.SEV = self.S("sev")

    def S(self, name):
        if name not in self.sems:
            self.sems[name] = SemT(self.es, self.nc, name)
        return self.sems[name]

    def sb(self, name, shape, dt=F32):
        return self.es.enter_context(self.nc.sbuf_tensor(name, shape, dt))

    def v(self, fn, *deps, ev=False):
        self.p.wait("vector", self.SV)
        for d in deps:
            self.p.wait("vector", d)
        self.p.op("vector", fn, incs=[(self.SV, 1)])

    def a(self, fn, *deps, ev=False):
        self.p.wait("scalar", self.SA)
        self.p.wait("scalar", self.SV, getattr(self, "const_done", 0))
        for d in deps:
            self.p.wait("scalar", d)
        self.p.op("scalar", fn, incs=[(self.SA, 1)])

    def bwait(self, eng, rec):
        if rec:
            self.p.wait(eng, rec[0], rec[1])

    def gate(self):
        for eng in Prog.ENG:
            for sem in list(self.sems.values()):
                self.p.wait(eng, sem)

    def t(self, fn, *deps, sp=True, extra=()):
        for d in deps:
            self.p.wait("tensor", d)
        self.p.op("tensor", fn, incs=[(self.SP, 1)])

    def g(self, fn, *deps):
        self.p.wait("gpsimd", self.SG)
        for d in deps:
            self.p.wait("gpsimd", d)
        self.p.op("gpsimd", fn, incs=[(self.SG, 1)])

    def dma(self, eng, fn, sem, *deps, extra=()):
        for d in deps:
            self.p.wait(eng, d)
        self.p.op(eng, fn, incs=[(sem, 16)] + list(extra))

    def _mk_dram(self):
        nc = self.nc
        T = self.T
        dk = "ExternalOutput" if self.debug else "Internal"

        def I_(name, shape, dtype=F32):
            return nc.dram_tensor(name, shape, dtype, kind="ExternalInput")
        self.x_fm = I_("x_fm", [DM, T + 3])
        self.winT = I_("winT", [DM, PROJ])
        self.woutT = I_("woutT", [ZD, DM])
        self.wgT = I_("wgT", [DM, HID])
        self.wuT = I_("wuT", [DM, HID])
        self.wdT = I_("wdT", [HID, DM])
        self.rms_col = I_("rms_col", [128, KC])
        self.norm_col = I_("norm_col", [128, ZC])
        self.conv_w = I_("conv_w", [128, XCC, 4])
        self.conv_b = I_("conv_b", [128, XCC])
        self.dt_bias = I_("dt_bias", [64, 1])
        self.a_log = I_("a_log", [64, 1])
        self.d_rep = I_("d_rep", [128, ZC])
        self.idm = I_("idm", [128, 128])
        self.tril = I_("tril", [128, 512], BF16)
        self.mask_v = I_("mask_v", [128, 3])
        self.mask_u = I_("mask_u", [1, 3])
        self.mask_u1m = I_("mask_u1m", [1, 3])

        self.out_d = nc.dram_tensor("out_d", [DM, T], F32, kind="ExternalOutput")
        self.z_d = nc.dram_tensor("z_d", [ZD, T], F32, kind=dk)
        self.xbc_d = nc.dram_tensor("xbc_d", [XBC, T + 3], F32, kind=dk)
        self.xcbf_d = nc.dram_tensor("xcbf_d", [XBC, T], BF16, kind=dk)
        self.xsf_d = nc.dram_tensor("xsf_d", [ZD, T], F32, kind=dk)
        self.bbf_d = nc.dram_tensor("bbf_d", [128, T], BF16, kind=dk)
        self.dacr_d = nc.dram_tensor("dacr_d", [64, T], F32, kind=dk)
        self.erow_d = nc.dram_tensor("erow_d", [64, T], BF16, kind=dk)
        self.yn_d = nc.dram_tensor("yn_d", [ZD, T], BF16, kind=dk)
        self.x2_d = nc.dram_tensor("x2_d", [DM, T], F32, kind=dk)
        self.rowm_d = nc.dram_tensor("rowm_d", [1, T], F32, kind=dk)
        self.rowr_d = nc.dram_tensor("rowr_d", [1, T], F32, kind=dk)
        self.hid_d = nc.dram_tensor("hid_d", [HID, T], BF16, kind=dk)
        self.down_d = nc.dram_tensor("down_d", [DM, T], F32, kind=dk)
        self.cc_in = nc.dram_tensor("cc_in", [128, NH * HP], BF16)
        self.cc_out = nc.dram_tensor("cc_out", [self.grp * 128, NH * HP], BF16)
        self.ccd_in = nc.dram_tensor("ccd_in", [1, 64], F32)
        self.ccd_out = nc.dram_tensor("ccd_out", [self.grp, 64], F32)
        if self.debug:
            self.carry_dbg = nc.dram_tensor("carry_dbg", [128, NH * HP], F32,
                                            kind="ExternalOutput")
            self.sin_dbg = nc.dram_tensor("sin_dbg", [128, NH * HP], F32,
                                          kind="ExternalOutput")

    def _mk_sbuf(self):
        sb = self.sb
        T = self.T
        self.pb = [self.es.enter_context(self.nc.psum_tensor(f"pb{i}", [128, 512], F32))
                   for i in range(6)]
        self.ph = self.es.enter_context(self.nc.psum_tensor("ph", [3, 512], F32))
        self.pstat = self.es.enter_context(self.nc.psum_tensor("pstat", [1, 512], F32))

        self.ones_col = sb("ones_col", [128, 1])
        self.ones_row = sb("ones_row", [1, 128])
        self.ident = sb("ident", [128, 128])
        self.rms_c = sb("rms_c", [128, KC])
        self.norm_c = sb("norm_c", [128, ZC])
        self.cw = sb("cw", [128, XCC, 4])
        self.cb = sb("cb", [128, XCC])
        self.dtb = sb("dtb", [64, 1])
        self.negA = sb("negA", [64, 1])
        self.drep = sb("drep", [128, ZC])
        self.trilm = sb("trilm", [128, 512], BF16)
        self.mv = sb("mv", [128, 3])
        self.mu = sb("mu", [1, 3])
        self.mu1m = sb("mu1m", [1, 3])
        self.z64 = sb("z64", [64, CS])
        self.cst = sb("cst", [128, 8])
        self.rowbuf = sb("rowbuf", [1, 6 * WT])
        self.sumx = sb("sumx", [128, WT])
        self.sumx2 = sb("sumx2", [128, WT])
        self.sqst = sb("sqst", [128, 2, WT])
        self.lnt = sb("lnt", [128, WT])
        self.esM = contextlib.ExitStack()
        sm = lambda n, shape, dt=F32: self.esM.enter_context(self.nc.sbuf_tensor(n, shape, dt))
        self.dtr = sm("dtr", [64, T])
        self.dac_tm = sm("dac_tm", [128, self.NCH, 2, 64])
        self.dt_tm = sm("dt_tm", [128, self.NCH, 2, 64])
        self.g_tm = sm("g_tm", [128, self.NCH, 2, 64])
        self.dec_b = sm("dec_b", [128, self.NCH, 64])
        self.b_bf = sm("b_bf", [128, T], BF16)
        self.c_bf = sm("c_bf", [128, T], BF16)
        self.carry = sm("carry", [128, NH * HP])

    def phase_consts(self):
        CLD = self.S("c_ld")
        pairs = [(self.ident, self.idm), (self.rms_c, self.rms_col),
                 (self.norm_c, self.norm_col), (self.cw, self.conv_w),
                 (self.cb, self.conv_b), (self.dtb, self.dt_bias),
                 (self.negA, self.a_log), (self.drep, self.d_rep),
                 (self.trilm, self.tril), (self.mv, self.mask_v),
                 (self.mu, self.mask_u), (self.mu1m, self.mask_u1m)]
        for dst, src in pairs:
            self.dma("sync", (lambda dst, src: lambda e: e.dma_start(
                out=dst[:], in_=src.ap()))(dst, src), CLD)
        self.v(lambda e: e.memset(self.ones_col[:], 1.0))
        self.v(lambda e: e.memset(self.ones_row[:], 1.0))
        self.v(lambda e: e.memset(self.z64[:], 0.0))
        self.v(lambda e: e.memset(self.carry[:], 0.0))
        constants = [1e-5, 1.0, -1.0, -0.5, 1.0 / 2048, 1.0 / 4096]
        for i, val in enumerate(constants):
            self.v((lambda i, val: lambda e: e.memset(self.cst[:, i:i + 1], val))(i, val))
            self.nc.const_aps.register(F32, val, self.cst[:, i:i + 1])
        self.const_done = self.SV.n
        self.a(lambda e: e.activation(self.negA[:], self.negA[:], AF.Exp), CLD)
        self.v(lambda e: e.tensor_single_scalar(self.negA[:], self.negA[:], -1.0, ALU.mult),
               self.SA)
        self.CLD = CLD

    def ln_stats(self, W, nsrc, src, kind, src_deps=()):
        p, v, a, t = self.p, self.v, self.a, self.t
        rb, pstat = self.rowbuf, self.pstat
        SV, SA, SP, SEV = self.SV, self.SA, self.SP, self.SEV
        for k in range(nsrc):
            a(lambda e, k=k: e.activation(self.sqst[:, k % 2, 0:W], src(k), AF.Square),
              *(tuple(src_deps) + (SV,)))
            if k == 0:
                v(lambda e: e.tensor_copy(self.sumx[:, 0:W], src(0)), *src_deps)
                v(lambda e: e.tensor_copy(self.sumx2[:, 0:W], self.sqst[:, 0, 0:W]), SA)
            else:
                v(lambda e, k=k: e.tensor_add(self.sumx[:, 0:W], self.sumx[:, 0:W], src(k)))
                v(lambda e, k=k: e.tensor_add(self.sumx2[:, 0:W], self.sumx2[:, 0:W],
                                              self.sqst[:, k % 2, 0:W]), SA)
        sc = 1.0 / (128 * nsrc)
        t(lambda e: e.matmul(pstat[0:1, 0:W], self.ones_col[:], self.sumx[:, 0:W],
                             start=True, stop=True), SV, SA)
        a(lambda e: e.activation(rb[0:1, 0:W], pstat[0:1, 0:W], AF.Copy, scale=sc),
          SP)
        t(lambda e: e.matmul(pstat[0:1, 0:W], self.ones_col[:], self.sumx2[:, 0:W],
                             start=True, stop=True), SA, SV)
        a(lambda e: e.activation(rb[0:1, WT:WT + W], pstat[0:1, 0:W], AF.Copy, scale=sc),
          SP)
        a(lambda e: e.activation(rb[0:1, 2 * WT:2 * WT + W], rb[0:1, 0:W], AF.Square))
        v(lambda e: e.tensor_sub(rb[0:1, 2 * WT:2 * WT + W], rb[0:1, WT:WT + W],
                                 rb[0:1, 2 * WT:2 * WT + W]), SA)
        a(lambda e: e.activation(rb[0:1, 3 * WT:3 * WT + W], rb[0:1, 2 * WT:2 * WT + W],
                                 AF.Ln, bias=1e-5), SV)
        a(lambda e: e.activation(rb[0:1, 3 * WT:3 * WT + W], rb[0:1, 3 * WT:3 * WT + W],
                                 AF.Exp, scale=-0.5))
        if kind == "ln_rms":
            a(lambda e: e.activation(rb[0:1, 4 * WT:4 * WT + W], rb[0:1, 3 * WT:3 * WT + W],
                                     AF.Square))
            v(lambda e: e.tensor_mul(rb[0:1, 4 * WT:4 * WT + W], rb[0:1, 4 * WT:4 * WT + W],
                                     rb[0:1, 2 * WT:2 * WT + W]), SA)
            a(lambda e: e.activation(rb[0:1, 4 * WT:4 * WT + W], rb[0:1, 4 * WT:4 * WT + W],
                                     AF.Ln, bias=EPS), SV)
            a(lambda e: e.activation(rb[0:1, 4 * WT:4 * WT + W], rb[0:1, 4 * WT:4 * WT + W],
                                     AF.Exp, scale=-0.5))
            v(lambda e: e.tensor_mul(rb[0:1, 4 * WT:4 * WT + W], rb[0:1, 4 * WT:4 * WT + W],
                                     rb[0:1, 3 * WT:3 * WT + W]), SA)
            a_off = 4 * WT
        else:
            a_off = 3 * WT
        v(lambda e: e.scalar_tensor_tensor(rb[0:1, 5 * WT:5 * WT + W], rb[0:1, 0:W], -1.0,
                                           rb[0:1, a_off:a_off + W], ALU.mult, ALU.mult),
          SA)
        t(lambda e: e.matmul(self.pb[4][:, 0:W], self.ones_row[:], rb[0:1, a_off:a_off + W],
                             start=True, stop=True), SV, SA)
        t(lambda e: e.matmul(self.pb[5][:, 0:W], self.ones_row[:],
                             rb[0:1, 5 * WT:5 * WT + W], start=True, stop=True), SA, SV)

    def ln_apply(self, W, nsrc, src, dst, mulcol=None, src_deps=()):
        v = self.v
        for k in range(nsrc):
            v(lambda e, k=k: e.tensor_mul(self.lnt[:, 0:W], src(k), self.pb[4][:, 0:W]),
              *(tuple(src_deps) + (self.SP,)))
            if mulcol is not None:
                v(lambda e, k=k: e.tensor_add(self.lnt[:, 0:W], self.lnt[:, 0:W],
                                              self.pb[5][:, 0:W]))
                v(lambda e, k=k: e.tensor_single_scalar(dst(k), self.lnt[:, 0:W],
                                                        mulcol(k), ALU.mult))
            else:
                v(lambda e, k=k: e.tensor_add(dst(k), self.lnt[:, 0:W], self.pb[5][:, 0:W]))

    def phase_a(self):
        p, v, a, t, g, dma, S = self.p, self.v, self.a, self.t, self.g, self.dma, self.S
        nc, sb, es = self.nc, self.sb, self.es
        SV, SA, SP, SG, SEV, CLD = self.SV, self.SA, self.SP, self.SG, self.SEV, self.CLD
        T, NT, NCH = self.T, self.NT, self.NCH
        pb, ph, pstat = self.pb, self.ph, self.pstat
        esA = contextlib.ExitStack()
        sbA = lambda name, shape, dt=F32: esA.enter_context(nc.sbuf_tensor(name, shape, dt))

        LXH = S("lxh")
        h_halo = sbA("h_halo", [128, KC, 3], BF16)
        xh = sbA("xh", [128, KC, 3])
        for k in range(KC):
            dma("sync", (lambda k: lambda e: e.dma_start(
                out=xh[:, k, :], in_=self.x_fm[k * 128:(k + 1) * 128, 0:3]))(k), LXH)
        self.ln_stats(3, KC, lambda k: xh[:, k, :], "ln_rms", src_deps=(LXH,))
        self.ln_apply(3, KC, lambda k: xh[:, k, :], lambda k: h_halo[:, k, :],
                      mulcol=lambda k: self.rms_c[:, k:k + 1], src_deps=(CLD,))

        whs = [sbA(f"whs{i}", [128, 512], BF16) for i in range(4)]
        LWH = [S(f"lwh{i}") for i in range(4)]
        whs_free = {}
        hst = sbA("hst", [3, 512])
        xbst = sbA("xbst", [128, 3])
        STH = S("sth")
        for j in range(9):
            wj = 512 if j < 8 else 256
            for k in range(KC):
                idx = j * KC + k
                sl = idx % 4
                if idx >= 4:
                    self.bwait("gpsimd", whs_free.get(sl))
                dma("gpsimd", (lambda j, k, sl, wj: lambda e: e.dma_start(
                    out=whs[sl][:, 0:wj],
                    in_=self.winT[k * 128:(k + 1) * 128,
                                  4096 + j * 512:4096 + j * 512 + wj]))(j, k, sl, wj),
                    LWH[sl])
                deps = [LWH[sl]]
                if k == 0:
                    deps += [SV]
                t((lambda j, k, wj, sl: lambda e: e.matmul(
                    ph[0:3, 0:wj], h_halo[:, k, :], whs[sl][:, 0:wj],
                    start=(k == 0), stop=(k == KC - 1)))(j, k, wj, sl), *deps)
                whs_free[sl] = (SP, SP.n)
            v((lambda wj: lambda e: e.tensor_copy(hst[0:3, 0:wj], ph[0:3, 0:wj]))(wj),
              SP)
            for s2 in range(wj // 128):
                t((lambda s2: lambda e: e.transpose(
                    pb[5][:, 0:3], hst[0:3, s2 * 128:(s2 + 1) * 128],
                    self.ident[0:3, 0:3]))(s2), SV, SA, CLD)
                v(lambda e: e.tensor_copy(xbst[:], pb[5][:, 0:3]), SP, STH)
                dma("sync", (lambda j, s2: lambda e: e.dma_start(
                    out=self.xbc_d[j * 512 + s2 * 128:j * 512 + (s2 + 1) * 128, 0:3],
                    in_=xbst[:]))(j, s2), STH, SV)

        h_sb = sbA("h_sb", [128, KC, T], BF16)
        xt = [sbA(f"xt{i}", [128, KC, 256]) for i in range(2)]
        LXT = [S(f"lxt{i}") for i in range(2)]
        NLN = T // 256
        self._xt_done = [0, 0]
        for ti in range(NLN):
            sl = ti % 2
            if ti >= 2:
                p.wait("sync", SV, self._xt_done[sl])
            for k in range(KC):
                dma("sync", (lambda ti, k, sl: lambda e: e.dma_start(
                    out=xt[sl][:, k, :],
                    in_=self.x_fm[k * 128:(k + 1) * 128, 3 + ti * 256:3 + (ti + 1) * 256]))(ti, k, sl),
                    LXT[sl])
            self.ln_stats(256, KC, (lambda sl: lambda k: xt[sl][:, k, :])(sl), "ln_rms",
                          src_deps=(LXT[sl],))
            SL1 = S("sl1")
            dma("sync", (lambda ti: lambda e: e.dma_start(
                out=self.rowm_d[0:1, ti * 256:(ti + 1) * 256],
                in_=self.rowbuf[0:1, 0:256]))(ti), SL1, SA)
            dma("sync", (lambda ti: lambda e: e.dma_start(
                out=self.rowr_d[0:1, ti * 256:(ti + 1) * 256],
                in_=self.rowbuf[0:1, 3 * WT:3 * WT + 256]))(ti), SL1, SA)
            p.wait("scalar", SL1, SL1.n)
            p.wait("vector", SL1, SL1.n)
            self.ln_apply(256, KC, (lambda sl: lambda k: xt[sl][:, k, :])(sl),
                          (lambda ti: lambda k: h_sb[:, k, ti * 256:(ti + 1) * 256])(ti),
                          mulcol=lambda k: self.rms_c[:, k:k + 1])
            self._xt_done[sl] = SV.n
        self.h_done = SV.n

        wp = [sbA(f"wp{i}", [128, KC, 384], BF16) for i in range(2)]
        LWP = [S(f"lwp{i}") for i in range(2)]
        wp_free = {}
        zst = [sbA(f"zst{i}", [128, 512]) for i in range(2)]
        SZT = [S(f"szt{i}") for i in range(2)]
        xst = [sbA(f"xst{i}", [128, 512]) for i in range(2)]
        SXT = [S(f"sxt{i}") for i in range(2)]
        bank_free = {}
        MGS = [3] * 22 + [1]
        zu = [0, 0]
        xu = [0, 0]
        for mg in range(23):
            nm = MGS[mg]
            sl = mg % 2
            if mg >= 2:
                self.bwait("gpsimd", wp_free.get(sl))
            for k in range(KC):
                mw = nm * 128 if mg < 22 else 64
                dma("gpsimd", (lambda mg, k, sl, mw: lambda e: e.dma_start(
                    out=wp[sl][:, k, 0:mw],
                    in_=self.winT[k * 128:(k + 1) * 128, mg * 384:mg * 384 + mw]))(mg, k, sl, mw),
                    LWP[sl])
            for ti in range(NT):
                par = ti % 2
                for k in range(KC):
                    for m in range(nm):
                        mi = mg * 3 + m
                        bank = par * 3 + m
                        mw = 128 if mi < 66 else 64
                        deps = [LWP[sl]]
                        if k == 0:
                            self.bwait("tensor", bank_free.get(bank))
                            p.wait("tensor", SV, self.h_done)
                        t((lambda k, m, mi, mw, ti, sl, bank: lambda e: e.matmul(
                            pb[bank][0:mw, 0:512],
                            wp[sl][:, k, m * 128:m * 128 + mw],
                            h_sb[:, k, ti * WT:(ti + 1) * WT],
                            start=(k == 0), stop=(k == KC - 1)))(k, m, mi, mw, ti, sl, bank),
                          *deps)
                for m in range(nm):
                    mi = mg * 3 + m
                    bank = par * 3 + m
                    if mi < 32:
                        zsl = (zu[0]) % 2
                        zu[0] += 1
                        a((lambda bank, zsl: lambda e: e.activation(
                            zst[zsl][:], pb[bank][:, 0:512], AF.Silu))(bank, zsl),
                          SP, SZT[zsl])
                        dma("sync", (lambda mi, ti, zsl: lambda e: e.dma_start(
                            out=self.z_d[mi * 128:(mi + 1) * 128, ti * WT:(ti + 1) * WT],
                            in_=zst[zsl][:]))(mi, ti, zsl), SZT[zsl], SA)
                    elif mi < 66:
                        xsl = (xu[0]) % 2
                        xu[0] += 1
                        v((lambda bank, xsl: lambda e: e.tensor_copy(
                            xst[xsl][:], pb[bank][:, 0:512]))(bank, xsl),
                          SP, SXT[xsl])
                        dma("sync", (lambda mi, ti, xsl: lambda e: e.dma_start(
                            out=self.xbc_d[(mi - 32) * 128:(mi - 31) * 128,
                                           3 + ti * WT:3 + (ti + 1) * WT],
                            in_=xst[xsl][:]))(mi, ti, xsl), SXT[xsl], SV)
                    else:
                        v((lambda bank, ti: lambda e: e.tensor_copy(
                            self.dtr[:, ti * WT:(ti + 1) * WT], pb[bank][0:64, 0:512]))(bank, ti),
                          SP)
                    if mi < 32:
                        bank_free[bank] = (SA, SA.n)
                    else:
                        bank_free[bank] = (SV, SV.n)
                if ti == NT - 1:
                    wp_free[sl] = (SP, SP.n)
        self.esA1 = esA

    def _states_pass(self, c, xsT, gxsT, BT, LTR, LBT, bank_free):
        p, v, t, S = self.p, self.v, self.t, self.S
        pb = self.pb
        SV, SP, SEV = self.SV, self.SP, self.SEV
        for half in range(2):
            for q in range(2):
                v((lambda c, q, half: lambda e: e.tensor_tensor(
                    gxsT[:, q, :], xsT[:, q, half * 2048:(half + 1) * 2048],
                    bass.AP(self.g_tm, (c * 2 + q) * 64 + half * 32,
                            [[self.NCH * 2 * 64, 128], [1, 32], [0, 64]]),
                    ALU.mult))(c, q, half), LTR, SP)
            for q in range(2):
                for hb in range(4):
                    deps = [LBT, SV] if (q == 0 and hb == 0) else []
                    if q == 0:
                        self.bwait("tensor", bank_free.get(("st", hb)))
                    t((lambda half, q, hb: lambda e: e.matmul(
                        pb[hb][:, 0:512], BT[:, q, :],
                        gxsT[:, q, hb * 512:(hb + 1) * 512],
                        start=(q == 0), stop=(q == 1)))(half, q, hb),
                      *deps, sp=(q == 1))
            for hb in range(4):
                for h8 in range(8):
                    h = half * 32 + hb * 8 + h8
                    v((lambda c, h, hb, h8: lambda e: e.scalar_tensor_tensor(
                        self.carry[:, h * 64:(h + 1) * 64],
                        self.carry[:, h * 64:(h + 1) * 64],
                        self.dec_b[:, c, h:h + 1],
                        pb[hb][:, h8 * 64:(h8 + 1) * 64],
                        ALU.mult, ALU.add))(c, h, hb, h8),
                      *( [self.SP] if (hb == 0 and h8 == 0) else [] ))
                bank_free[("st", hb)] = (self.SV, self.SV.n)

    def phase_a2(self):
        p, v, a, t, g, dma, S = self.p, self.v, self.a, self.t, self.g, self.dma, self.S
        nc = self.nc
        SV, SA, SP, SG, SEV, CLD = self.SV, self.SA, self.SP, self.SG, self.SEV, self.CLD
        T, NT, NCH = self.T, self.NT, self.NCH
        pb, pstat = self.pb, self.pstat
        self.esA1.close()
        self.gate()
        esA2 = contextlib.ExitStack()
        sbA = lambda name, shape, dt=F32: esA2.enter_context(nc.sbuf_tensor(name, shape, dt))

        cvin = [sbA(f"cvin{i}", [128, 515]) for i in range(3)]
        LCV = [S(f"lcv{i}") for i in range(3)]
        cvac = [sbA(f"cvac{i}", [128, 512]) for i in range(2)]
        cvout = [sbA(f"cvout{i}", [128, 512]) for i in range(3)]
        SCB = S("scb")
        SCS = S("scs")
        SBB = S("sbb")
        cvu = {}
        SXT0, SXT1, STH = S("sxt0"), S("sxt1"), S("sth")
        for ti in range(NT):
            for mc in range(XCC):
                idx = ti * XCC + mc
                sl = idx % 3
                osl = idx % 3
                if idx >= 3:
                    p.wait("sync", SV, cvu.get(("in", sl), 0))
                dma("sync", (lambda ti, mc, sl: lambda e: e.dma_start(
                    out=cvin[sl][:],
                    in_=self.xbc_d[mc * 128:(mc + 1) * 128, ti * WT:ti * WT + 515]))(ti, mc, sl),
                    LCV[sl], SXT0, SXT1, STH)
                asl = idx % 2
                v((lambda mc, sl, asl: lambda e: e.tensor_scalar(
                    cvac[asl][:], cvin[sl][:, 0:512], self.cw[:, mc, 0:1], None,
                    ALU.mult))(mc, sl, asl), LCV[sl], CLD, SA)
                for kk in range(1, 4):
                    v((lambda mc, sl, asl, kk: lambda e: e.scalar_tensor_tensor(
                        cvac[asl][:], cvin[sl][:, kk:512 + kk], self.cw[:, mc, kk:kk + 1],
                        cvac[asl][:], ALU.mult, ALU.add))(mc, sl, asl, kk))
                cvu[("in", sl)] = SV.n
                p.wait("scalar", SCB)
                p.wait("scalar", SCS)
                p.wait("scalar", SBB)
                p.wait("scalar", SV)
                a((lambda mc, asl, osl: lambda e: e.activation(
                    cvout[osl][:], cvac[asl][:], AF.Silu, bias=self.cb[:, mc:mc + 1]))(mc, asl, osl),
                  SV)
                dma("gpsimd", (lambda ti, mc, osl: lambda e: e.dma_start(
                    out=self.xcbf_d[mc * 128:(mc + 1) * 128, ti * WT:(ti + 1) * WT],
                    in_=cvout[osl][:]))(ti, mc, osl), SCB, SA)
                if mc < 32:
                    dma("sync", (lambda ti, mc, osl: lambda e: e.dma_start(
                        out=self.xsf_d[mc * 128:(mc + 1) * 128, ti * WT:(ti + 1) * WT],
                        in_=cvout[osl][:]))(ti, mc, osl), SCS, SA)
                elif mc == 32:
                    v((lambda ti, osl: lambda e: e.tensor_copy(
                        self.b_bf[:, ti * WT:(ti + 1) * WT], cvout[osl][:]))(ti, osl), SA)
                    dma("gpsimd", (lambda ti, osl: lambda e: e.dma_start(
                        out=self.bbf_d[:, ti * WT:(ti + 1) * WT], in_=cvout[osl][:]))(ti, osl),
                        SBB, SA)
                else:
                    v((lambda ti, osl: lambda e: e.tensor_copy(
                        self.c_bf[:, ti * WT:(ti + 1) * WT], cvout[osl][:]))(ti, osl), SA)

        daw = sbA("daw", [64, T])
        av = sbA("av", [64, T])
        dts = self.dts = sbA("dts", [64, T])
        dac = self.dac = sbA("dac", [64, T])
        ehm = self.ehm = sbA("ehm", [64, T])
        ghm = self.ghm = sbA("ghm", [64, T])
        v(lambda e: e.tensor_scalar(dts[:], self.dtr[:], self.dtb[:, 0:1], None, ALU.add),
          CLD)
        v(lambda e: e.tensor_scalar(av[:], dts[:], -1.0, None, ALU.mult))
        v(lambda e: e.tensor_max(av[:], av[:], dts[:]))
        a(lambda e: e.activation(av[:], av[:], AF.Exp, scale=-1.0), SV)
        a(lambda e: e.activation(av[:], av[:], AF.Ln, bias=1.0))
        v(lambda e: e.tensor_scalar(dts[:], dts[:], 0.0, None, ALU.max))
        v(lambda e: e.tensor_add(dts[:], dts[:], av[:]), SA)
        v(lambda e: e.tensor_scalar(daw[:], dts[:], self.negA[:, 0:1], None, ALU.mult))
        for c in range(NCH):
            v((lambda c: lambda e: e.tensor_tensor_scan(
                dac[:, c * CS:(c + 1) * CS], daw[:, c * CS:(c + 1) * CS],
                self.z64[:], 0.0, ALU.add, ALU.add))(c))
        a(lambda e: e.activation(ehm[:], dac[:], AF.Exp), SV)
        DMR = S("dmr")
        dma("sync", lambda e: e.dma_start(out=self.dacr_d.ap(), in_=dac[:]), DMR, SV)
        dma("gpsimd", lambda e: e.dma_start(out=self.erow_d.ap(), in_=ehm[:]), S("dmg"), SA)
        for c in range(NCH):
            a((lambda c: lambda e: e.activation(
                ghm[:, c * CS:(c + 1) * CS], dac[:, c * CS:(c + 1) * CS], AF.Exp,
                scale=-1.0, bias=dac[:, c * CS + CS - 1:c * CS + CS]))(c), SV)
        v(lambda e: e.tensor_mul(ghm[:], ghm[:], dts[:]), SA)
        decrow = sbA("decrow", [1, 64])
        for c in range(NCH):
            for (src, dst) in ((dac, self.dac_tm), (dts, self.dt_tm), (ghm, self.g_tm)):
                for q in range(2):
                    t((lambda src, c, q: lambda e: e.transpose(
                        pb[5][:, 0:64], src[:, c * CS + q * 128:c * CS + (q + 1) * 128],
                        self.ident[0:64, 0:64]))(src, c, q), SV, SA, CLD)
                    v((lambda dst, c, q: lambda e: e.tensor_copy(
                        dst[:, c, q, :], pb[5][:, 0:64]))(dst, c, q), SP)
            t((lambda c: lambda e: e.transpose(
                pstat[0:1, 0:64], ehm[:, c * CS + CS - 1:c * CS + CS],
                self.ident[0:64, 0:64]))(c), SV, SA)
            v(lambda e: e.tensor_copy(decrow[:], pstat[0:1, 0:64]), SP)
            t(lambda e: e.matmul(pb[5][:, 0:64], self.ones_row[:], decrow[:],
                                 start=True, stop=True), SV, SA)
            v((lambda c: lambda e: e.tensor_copy(self.dec_b[:, c, :], pb[5][:, 0:64]))(c),
              SP)
        dtot = sbA("dtot", [64, 1])
        dtotrow = sbA("dtotrow", [1, 64])
        v(lambda e: e.reduce_sum(dtot[:], bass.AP(dac, CS - 1,
            [[self.T, 64], [CS, NCH]]), axis=mybir.AxisListType.X), SV)
        a(lambda e: e.activation(dtot[:], dtot[:], AF.Exp), SV)
        t(lambda e: e.transpose(pstat[0:1, 0:64], dtot[:], self.ident[0:64, 0:64]),
          SA, SV)
        v(lambda e: e.tensor_copy(dtotrow[:], pstat[0:1, 0:64]), SP)
        SCC = S("scc")
        SCG = S("scg")
        dma("sync", lambda e: e.dma_start(out=self.ccd_in.ap(), in_=dtotrow[:]), SCC, SV)

        xsT = sbA("xsT", [128, 2, ZD], BF16)
        gxsT = sbA("gxsT", [128, 2, 2048], BF16)
        BT = sbA("BT", [128, 2, 128], BF16)
        LTR = S("ltr")
        LBT = S("lbt")
        bank_free = {}
        for c in range(NCH):
            for cc in range(32):
                for q in range(2):
                    dma("sync", (lambda c, cc, q: lambda e: e.dma_start(
                        out=xsT[:, q, cc * 128:(cc + 1) * 128],
                        in_=self.xcbf_d[cc * 128:(cc + 1) * 128,
                                        c * CS + q * 128:c * CS + (q + 1) * 128],
                        transpose=True))(c, cc, q), LTR, SCB, SG)
            for q in range(2):
                dma("sync", (lambda c, q: lambda e: e.dma_start(
                    out=BT[:, q, :],
                    in_=self.bbf_d[:, c * CS + q * 128:c * CS + (q + 1) * 128],
                    transpose=True))(c, q), LBT, SBB, SG)
            self._states_pass(c, xsT, gxsT, BT, LTR, LBT, bank_free)
            p.wait("sync", SP)

        if self.debug:
            DBG = S("dbg")
            dma("sync", lambda e: e.dma_start(out=self.carry_dbg.ap(), in_=self.carry[:]),
                DBG, SV)

        carrybf = sbA("carrybf", [128, NH * HP], BF16)
        v(lambda e: e.tensor_copy(carrybf[:], self.carry[:]))
        dma("gpsimd", lambda e: e.dma_start(out=self.cc_in.ap(), in_=carrybf[:]), SCG, SV)
        SCCD = S("sccd")
        groups = [[0, 1, 2, 3], [4, 5, 6, 7]] if self.grp == 4 else [list(range(8))]
        p.wait("gpsimd", SCC, SCC.n)
        p.wait("gpsimd", SCG, SCG.n)
        p.op("gpsimd", (lambda groups: lambda e: e.collective_compute(
            "AllGather", ALU.bypass, replica_groups=groups,
            ins=[self.cc_in.ap().opt()],
            outs=[self.cc_out.ap().opt()]))(groups),
            incs=[(SCCD, 1)])
        p.op("gpsimd", (lambda groups: lambda e: e.collective_compute(
            "AllGather", ALU.bypass, replica_groups=groups,
            ins=[self.ccd_in.ap().opt()],
            outs=[self.ccd_out.ap().opt()]))(groups),
            incs=[(SCCD, 1)])

        slotb = sbA("slotb", [128, NH * HP], BF16)
        scr = sbA("scr", [128, NH * HP])
        dcol = sbA("dcol", [1, 64])
        dtbc = sbA("dtbc", [128, 2, 64])
        LSL = S("lsl")
        p.wait("sync", SCCD, SCCD.n)
        dma("sync", lambda e: e.dma_start(out=slotb[:], in_=self.cc_out[0:128, :]), LSL)
        v(lambda e: e.tensor_scalar(self.carry[:], slotb[:], self.mv[:, 0:1], None,
                                    ALU.mult), LSL, S("dbg"))
        for j in (1, 2):
            dma("sync", (lambda j: lambda e: e.dma_start(
                out=dcol[:], in_=self.ccd_out[j:j + 1, :]))(j), LSL, SV)
            v((lambda j: lambda e: e.scalar_tensor_tensor(
                dcol[:], dcol[:], self.mu[0:1, j:j + 1],
                bass.AP(self.mu1m, j, [[3, 1], [0, 64]]), ALU.mult, ALU.add))(j), LSL)
            t(lambda e: e.matmul(pb[4][:, 0:64], self.ones_row[:], dcol[:],
                                 start=True, stop=True), SV, SA)
            v((lambda j: lambda e: e.tensor_copy(dtbc[:, j - 1, :], pb[4][:, 0:64]))(j),
              SP)
            dma("sync", (lambda j: lambda e: e.dma_start(
                out=slotb[:], in_=self.cc_out[j * 128:(j + 1) * 128, :]))(j), LSL, SV)
            v((lambda j: lambda e: e.tensor_scalar(
                scr[:], slotb[:], self.mv[:, j:j + 1], None, ALU.mult))(j), LSL)
            v((lambda j: lambda e: e.tensor_tensor(
                self.carry[:], self.carry[:],
                bass.AP(dtbc, (j - 1) * 64, [[2 * 64, 128], [1, 64], [0, 64]]),
                ALU.mult))(j))
            v(lambda e: e.tensor_add(self.carry[:], self.carry[:], scr[:]))
        if self.debug:
            DBG = S("dbg")
            dma("sync", lambda e: e.dma_start(out=self.sin_dbg.ap(), in_=self.carry[:]),
                DBG, SV)
        self.esA2 = esA2

    def phase_b(self):
        p, v, a, t, g, dma, S = self.p, self.v, self.a, self.t, self.g, self.dma, self.S
        nc = self.nc
        SV, SA, SP, SG, SEV, CLD = self.SV, self.SA, self.SP, self.SG, self.SEV, self.CLD
        T, NT, NCH = self.T, self.NT, self.NCH
        pb, pstat = self.pb, self.pstat
        self.esA2.close()
        self.gate()
        esB = contextlib.ExitStack()
        sbB = lambda name, shape, dt=F32: esB.enter_context(nc.sbuf_tensor(name, shape, dt))

        z_sb = sbB("z_sb", [128, 32, 256])
        xs_sb = sbB("xs_sb", [128, 16, 256])
        y_sb = sbB("y_sb", [128, 32, 256])
        xsT = sbB("xsTb", [128, 2, ZD], BF16)
        gxsT = sbB("gxsTb", [128, 2, 2048], BF16)
        BT = sbB("BTb", [128, 2, 128], BF16)
        ps_bf = sbB("ps_bf", [128, NH * HP], BF16)
        ynv = lambda cc: bass.AP(xsT, cc * 256, [[2 * ZD, 128], [1, 256]])
        R = [sbB(f"Rb{i}", [128, 256]) for i in range(2)]
        EB = [sbB(f"EBb{i}", [128, 256], BF16) for i in range(2)]
        segm = [sbB(f"segm{i}", [128, 256]) for i in range(2)]
        Lb = [sbB(f"Lbb{i}", [128, 512], BF16) for i in range(2)]
        MT = [sbB(f"MTb{i}", [128, 512], BF16) for i in range(2)]
        Ce = [sbB(f"Ceb{i}", [128, 256], BF16) for i in range(2)]
        CBt = sbB("CBtb", [128, 512], BF16)
        sqstB = sbB("sqstB", [128, 2, 256])
        sumxB = sbB("sumxB", [128, 256])
        rowB = sbB("rowB", [1, 256])

        LZC, LXS, LTR2, LBT2 = S("lzc"), S("lxs"), S("ltr2"), S("lbt2")
        LR = [S(f"lrb{i}") for i in range(2)]
        LE = [S(f"leb{i}") for i in range(2)]
        SYN = S("syn")
        DMR = S("dmr")
        SCB = S("scb")
        SBB = S("sbb")
        SZT = [S("szt0"), S("szt1")]
        SCS = S("scs")
        bank_free = {}
        rcons = {}
        econs = {}
        cons_mt = {}
        cons_ce = {}
        zy_done = 0
        xs_done = 0
        yn_done = 0

        for c in range(NCH):
            for cc in range(32):
                p.wait("sync", SV, zy_done)
                dma("sync", (lambda c, cc: lambda e: e.dma_start(
                    out=z_sb[:, cc, :],
                    in_=self.z_d[cc * 128:(cc + 1) * 128, c * CS:(c + 1) * CS]))(c, cc),
                    LZC, SZT[0], SZT[1])
            for cc in range(16):
                p.wait("sync", SV, xs_done)
                dma("sync", (lambda c, cc: lambda e: e.dma_start(
                    out=xs_sb[:, cc, :],
                    in_=self.xsf_d[cc * 128:(cc + 1) * 128, c * CS:(c + 1) * CS]))(c, cc),
                    LXS, SCS)
            for cc in range(32):
                for q in range(2):
                    dma("sync", (lambda c, cc, q: lambda e: e.dma_start(
                        out=xsT[:, q, cc * 128:(cc + 1) * 128],
                        in_=self.xcbf_d[cc * 128:(cc + 1) * 128,
                                        c * CS + q * 128:c * CS + (q + 1) * 128],
                        transpose=True))(c, cc, q), LTR2, SCB, SP, SYN)
            for q in range(2):
                dma("sync", (lambda c, q: lambda e: e.dma_start(
                    out=BT[:, q, :],
                    in_=self.bbf_d[:, c * CS + q * 128:c * CS + (q + 1) * 128],
                    transpose=True))(c, q), LBT2, SBB, SP)
            v(lambda e: e.tensor_copy(ps_bf[:], self.carry[:]), SP)
            self.bwait("tensor", bank_free.get(4))
            for q in range(2):
                t((lambda c, q: lambda e: e.matmul(
                    pb[4][:, q * 256:(q + 1) * 256],
                    self.b_bf[:, c * CS + q * 128:c * CS + (q + 1) * 128],
                    self.c_bf[:, c * CS:(c + 1) * CS],
                    start=True, stop=True))(c, q))
            v(lambda e: e.tensor_tensor(CBt[:], pb[4][:, 0:512], self.trilm[:], ALU.mult),
              SP)
            bank_free[4] = (SV, SV.n)
            for h in range(NH):
                hsl = h % 2
                p.wait("sync", SV, rcons.get(hsl, 0))
                dma("sync", (lambda c, h, hsl: lambda e: e.dma_start(
                    out=R[hsl][:],
                    in_=bass.AP(self.dacr_d, h * T + c * CS, [[0, 128], [1, 256]])))(c, h, hsl),
                    LR[hsl], DMR)
                p.wait("sync", SV, econs.get(hsl, 0))
                dma("sync", (lambda c, h, hsl: lambda e: e.dma_start(
                    out=EB[hsl][:],
                    in_=bass.AP(self.erow_d, h * T + c * CS, [[0, 128], [1, 256]])))(c, h, hsl),
                    LE[hsl], S("dmg"))
                for q in range(2):
                    v((lambda c, h, hsl, q: lambda e: e.tensor_scalar(
                        segm[hsl][:], R[hsl][:],
                        self.dac_tm[:, c, q, h:h + 1], 0.0,
                        ALU.subtract, ALU.min))(c, h, hsl, q), LR[hsl], SA)
                    a((lambda hsl, q: lambda e: e.activation(
                        Lb[hsl][:, q * 256:(q + 1) * 256], segm[hsl][:], AF.Exp))(hsl, q),
                      SV)
                p.wait("vector", SP, cons_mt.get(hsl, 0))
                for q in range(2):
                    v((lambda c, h, hsl, q: lambda e: e.scalar_tensor_tensor(
                        MT[hsl][:, q * 256:(q + 1) * 256],
                        Lb[hsl][:, q * 256:(q + 1) * 256],
                        self.dt_tm[:, c, q, h:h + 1],
                        CBt[:, q * 256:(q + 1) * 256],
                        ALU.mult, ALU.mult))(c, h, hsl, q), SA)
                p.wait("vector", SP, cons_ce.get(hsl, 0))
                v((lambda c, hsl: lambda e: e.tensor_tensor(
                    Ce[hsl][:], self.c_bf[:, c * CS:(c + 1) * CS], EB[hsl][:],
                    ALU.mult))(c, hsl), LE[hsl])
                rcons[hsl] = SV.n
                econs[hsl] = SV.n
                yb = 4 + (h % 2)
                self.bwait("tensor", bank_free.get(yb))
                t((lambda h, hsl, yb: lambda e: e.matmul(
                    pb[yb][0:64, 0:256], xsT[:, 0, h * 64:(h + 1) * 64],
                    MT[hsl][:, 0:256], start=True, stop=False))(h, hsl, yb),
                  LTR2, SV)
                t((lambda h, hsl, yb: lambda e: e.matmul(
                    pb[yb][0:64, 0:256], xsT[:, 1, h * 64:(h + 1) * 64],
                    MT[hsl][:, 256:512], start=False, stop=False))(h, hsl, yb))
                t((lambda h, hsl, yb: lambda e: e.matmul(
                    pb[yb][0:64, 0:256], ps_bf[:, h * 64:(h + 1) * 64],
                    Ce[hsl][:], start=False, stop=True))(h, hsl, yb))
                cons_mt[hsl] = SP.n
                cons_ce[hsl] = SP.n
                base = 64 * (h % 2)
                cc = h // 2
                v((lambda h, base, cc, yb: lambda e: e.scalar_tensor_tensor(
                    y_sb[base:base + 64, cc, :], xs_sb[base:base + 64, cc % 16, :],
                    self.drep[base:base + 64, cc:cc + 1], pb[yb][0:64, 0:256],
                    ALU.mult, ALU.add))(h, base, cc, yb), SP, LXS)
                bank_free[yb] = (SV, SV.n)
                if h == 31:
                    for cc2 in range(16, 32):
                        p.wait("sync", SV, SV.n)
                        dma("sync", (lambda c, cc2: lambda e: e.dma_start(
                            out=xs_sb[:, cc2 - 16, :],
                            in_=self.xsf_d[cc2 * 128:(cc2 + 1) * 128,
                                           c * CS:(c + 1) * CS]))(c, cc2), LXS)
            xs_done = SV.n
            self._states_pass(c, xsT, gxsT, BT, LTR2, LBT2, bank_free)
            for cc in range(32):
                v((lambda cc: lambda e: e.tensor_mul(
                    y_sb[:, cc, :], y_sb[:, cc, :], z_sb[:, cc, :]))(cc),
                  *( [LZC] if cc == 0 else [] ))
            zy_done = SV.n
            for cc in range(32):
                a((lambda cc: lambda e: e.activation(
                    sqstB[:, cc % 2, :], y_sb[:, cc, :], AF.Square))(cc), SV)
                if cc == 0:
                    v(lambda e: e.tensor_copy(sumxB[:], sqstB[:, 0, :]), SA)
                else:
                    v((lambda cc: lambda e: e.tensor_add(
                        sumxB[:], sumxB[:], sqstB[:, cc % 2, :]))(cc), SA)
            t(lambda e: e.matmul(pstat[0:1, 0:256], self.ones_col[:], sumxB[:],
                                 start=True, stop=True), SV, SA)
            a(lambda e: e.activation(rowB[:], pstat[0:1, 0:256], AF.Copy,
                                     scale=1.0 / ZD), SP)
            a(lambda e: e.activation(rowB[:], rowB[:], AF.Ln, bias=EPS))
            a(lambda e: e.activation(rowB[:], rowB[:], AF.Exp, scale=-0.5))
            self.bwait("tensor", bank_free.get(4))
            t(lambda e: e.matmul(pb[4][:, 0:256], self.ones_row[:],
                                 rowB[:], start=True, stop=True), SA, SV)
            for cc in range(32):
                p.wait("vector", SYN, yn_done)
                v((lambda cc: lambda e: e.scalar_tensor_tensor(
                    ynv(cc), y_sb[:, cc, :], self.norm_c[:, cc:cc + 1],
                    pb[4][:, 0:256], ALU.mult, ALU.mult))(cc), SP)
            bank_free[4] = (SV, SV.n)
            for cc in range(32):
                dma("sync", (lambda c, cc: lambda e: e.dma_start(
                    out=self.yn_d[cc * 128:(cc + 1) * 128, c * CS:(c + 1) * CS],
                    in_=ynv(cc)))(c, cc), SYN, SV)
            yn_done = SYN.n
        self.esB = esB

    def phase_b2(self):
        p, v, a, t, g, dma, S = self.p, self.v, self.a, self.t, self.g, self.dma, self.S
        nc = self.nc
        SV, SA, SP, SG, SEV = self.SV, self.SA, self.SP, self.SG, self.SEV
        T, NT = self.T, self.NT
        pb = self.pb
        self.esB.close()
        self.esM.close()
        self.gate()
        self.esH3 = contextlib.ExitStack()
        h3 = self.h3 = self.esH3.enter_context(nc.sbuf_tensor("h3", [128, KC, T], BF16))
        esB2 = contextlib.ExitStack()
        sbB = lambda name, shape, dt=F32: esB2.enter_context(nc.sbuf_tensor(name, shape, dt))
        wo = sbB("wo", [128, 32, 512], BF16)
        yn_t = sbB("yn_t", [128, 32, 512], BF16)
        attn = sbB("attn", [128, 16, 512])
        xr = [sbB(f"xr{i}", [128, 512]) for i in range(2)]
        LWO, LYN, LX2, SX2 = S("lwo"), S("lyn"), S("lx2"), S("sx2")
        SYN = S("syn")
        bank_free = {}
        fwo = 0
        lyn_cons = 0
        for ti in range(NT):
            p.wait("sync", SP, lyn_cons)
            for k in range(32):
                dma("sync", (lambda ti, k: lambda e: e.dma_start(
                    out=yn_t[:, k, :],
                    in_=self.yn_d[k * 128:(k + 1) * 128, ti * WT:(ti + 1) * WT]))(ti, k),
                    LYN, SYN)
            for mg in range(4):
                p.wait("gpsimd", SP, fwo)
                for k in range(32):
                    dma("gpsimd", (lambda mg, k: lambda e: e.dma_start(
                        out=wo[:, k, :],
                        in_=self.woutT[k * 128:(k + 1) * 128,
                                       mg * 512:(mg + 1) * 512]))(mg, k), LWO)
                for k in range(32):
                    for m in range(4):
                        if k == 0:
                            self.bwait("tensor", bank_free.get(m))
                        t((lambda k, m: lambda e: e.matmul(
                            pb[m][:, 0:512], wo[:, k, m * 128:(m + 1) * 128],
                            yn_t[:, k, :], start=(k == 0), stop=(k == 31)))(k, m),
                          *( [LWO, LYN] if k == 0 else [] ))
                for m in range(4):
                    v((lambda mg, m: lambda e: e.tensor_copy(
                        attn[:, mg * 4 + m, :], pb[m][:, 0:512]))(mg, m), SP)
                    bank_free[m] = (SV, SV.n)
                fwo = SP.n
            lyn_cons = SP.n
            SL1 = S("sl1")
            LRW = S("lrw")
            dma("sync", (lambda ti: lambda e: e.dma_start(
                out=self.rowbuf[0:1, 0:512],
                in_=self.rowm_d[0:1, ti * WT:(ti + 1) * WT]))(ti), LRW, SL1, SA, SV)
            dma("sync", (lambda ti: lambda e: e.dma_start(
                out=self.rowbuf[0:1, 3 * WT:3 * WT + 512],
                in_=self.rowr_d[0:1, ti * WT:(ti + 1) * WT]))(ti), LRW)
            v(lambda e: e.scalar_tensor_tensor(
                self.rowbuf[0:1, 5 * WT:5 * WT + 512], self.rowbuf[0:1, 0:512], -1.0,
                self.rowbuf[0:1, 3 * WT:3 * WT + 512], ALU.mult, ALU.mult), LRW)
            t(lambda e: e.matmul(pb[4][:, 0:512], self.ones_row[:],
                                 self.rowbuf[0:1, 3 * WT:3 * WT + 512],
                                 start=True, stop=True), SV, SA)
            t(lambda e: e.matmul(pb[5][:, 0:512], self.ones_row[:],
                                 self.rowbuf[0:1, 5 * WT:5 * WT + 512],
                                 start=True, stop=True))
            for k in range(16):
                dma("sync", (lambda ti, k: lambda e: e.dma_start(
                    out=xr[k % 2][:],
                    in_=self.x_fm[k * 128:(k + 1) * 128,
                                  3 + ti * WT:3 + (ti + 1) * WT]))(ti, k), LX2, SV)
                v((lambda k: lambda e: e.tensor_mul(self.lnt[:, 0:512], xr[k % 2][:],
                                                    pb[4][:, 0:512]))(k), self.SP, LX2)
                v((lambda k: lambda e: e.tensor_add(self.lnt[:, 0:512], self.lnt[:, 0:512],
                                                    pb[5][:, 0:512]))(k))
                v((lambda k: lambda e: e.tensor_add(attn[:, k, :], attn[:, k, :],
                                                    self.lnt[:, 0:512]))(k))
            self.ln_stats(512, 16, (lambda ti: lambda k: attn[:, k, :])(ti), "ln")
            for k in range(16):
                dma("sync", (lambda ti, k: lambda e: e.dma_start(
                    out=xr[k % 2][:],
                    in_=self.x_fm[k * 128:(k + 1) * 128,
                                  3 + ti * WT:3 + (ti + 1) * WT]))(ti, k), LX2, SV)
                v((lambda k: lambda e: e.tensor_mul(self.lnt[:, 0:512], attn[:, k, :],
                                                    pb[4][:, 0:512]))(k), self.SP)
                v((lambda k: lambda e: e.tensor_add(self.lnt[:, 0:512], self.lnt[:, 0:512],
                                                    pb[5][:, 0:512]))(k))
                v((lambda k: lambda e: e.tensor_add(attn[:, k, :], self.lnt[:, 0:512],
                                                    xr[k % 2][:]))(k), LX2)
                dma("sync", (lambda ti, k: lambda e: e.dma_start(
                    out=self.x2_d[k * 128:(k + 1) * 128, ti * WT:(ti + 1) * WT],
                    in_=attn[:, k, :]))(ti, k), SX2, SV)
            self.ln_stats(512, 16, (lambda ti: lambda k: attn[:, k, :])(ti), "ln")
            self.ln_apply(512, 16, (lambda ti: lambda k: attn[:, k, :])(ti),
                          (lambda ti: lambda k: h3[:, k, ti * WT:(ti + 1) * WT])(ti))
        self.h3_done = SV.n
        esB2.close()

    def phase_c1(self):
        p, v, a, t, g, dma, S = self.p, self.v, self.a, self.t, self.g, self.dma, self.S
        nc = self.nc
        SV, SA, SP, SG, SEV = self.SV, self.SA, self.SP, self.SG, self.SEV
        T, NT = self.T, self.NT
        pb = self.pb
        self.gate()
        esC1 = contextlib.ExitStack()
        sbC = lambda name, shape, dt=F32: esC1.enter_context(nc.sbuf_tensor(name, shape, dt))
        wg = [sbC(f"wg{i}", [128, KC, 384], BF16) for i in range(2)]
        wu = [sbC(f"wu{i}", [128, KC, 384], BF16) for i in range(2)]
        sg = [sbC(f"sg{i}", [128, 512]) for i in range(2)]
        hidst = [sbC(f"hidst{i}", [128, 512], BF16) for i in range(3)]
        LWG = [S(f"lwg{i}") for i in range(2)]
        LWU = [S(f"lwu{i}") for i in range(2)]
        wg_free = {}
        SHD = [S(f"shd{i}") for i in range(3)]
        h3 = self.h3
        bank_free = {}
        sg_cons = [0, 0]
        MGS = [3] * 21 + [1]
        for mg in range(22):
            nm = MGS[mg]
            sl = mg % 2
            if mg >= 2:
                self.bwait("gpsimd", wg_free.get(sl))
            for k in range(KC):
                mw = nm * 128
                dma("gpsimd", (lambda mg, k, sl, mw: lambda e: e.dma_start(
                    out=wg[sl][:, k, 0:mw],
                    in_=self.wgT[k * 128:(k + 1) * 128, mg * 384:mg * 384 + mw]))(mg, k, sl, mw),
                    LWG[sl])
                dma("gpsimd", (lambda mg, k, sl, mw: lambda e: e.dma_start(
                    out=wu[sl][:, k, 0:mw],
                    in_=self.wuT[k * 128:(k + 1) * 128, mg * 384:mg * 384 + mw]))(mg, k, sl, mw),
                    LWU[sl])
            for ti in range(NT):
                for k in range(KC):
                    for m in range(nm):
                        if k == 0:
                            self.bwait("tensor", bank_free.get(m))
                            self.bwait("tensor", bank_free.get(3 + m))
                            p.wait("tensor", SV, self.h3_done)
                        t((lambda k, m, ti, sl: lambda e: e.matmul(
                            pb[m][:, 0:512], wg[sl][:, k, m * 128:(m + 1) * 128],
                            h3[:, k, ti * WT:(ti + 1) * WT],
                            start=(k == 0), stop=(k == KC - 1)))(k, m, ti, sl),
                          *( [LWG[sl], LWU[sl]] if k == 0 else [] ))
                        t((lambda k, m, ti, sl: lambda e: e.matmul(
                            pb[3 + m][:, 0:512], wu[sl][:, k, m * 128:(m + 1) * 128],
                            h3[:, k, ti * WT:(ti + 1) * WT],
                            start=(k == 0), stop=(k == KC - 1)))(k, m, ti, sl))
                for m in range(nm):
                    mi = mg * 3 + m
                    ssl = m % 2
                    p.wait("scalar", SV, sg_cons[ssl])
                    a((lambda m, ssl: lambda e: e.activation(
                        sg[ssl][:], pb[m][:, 0:512], AF.Silu))(m, ssl), SP)
                    hsl = (ti * 3 + m) % 3
                    p.wait("vector", SHD[hsl], SHD[hsl].n)
                    v((lambda m, ssl, hsl: lambda e: e.tensor_tensor(
                        hidst[hsl][:], sg[ssl][:], pb[3 + m][:, 0:512],
                        ALU.mult))(m, ssl, hsl), SA)
                    sg_cons[ssl] = SV.n
                    bank_free[m] = (SA, SA.n)
                    bank_free[3 + m] = (SV, SV.n)
                    dma("sync", (lambda mi, ti, hsl: lambda e: e.dma_start(
                        out=self.hid_d[mi * 128:(mi + 1) * 128, ti * WT:(ti + 1) * WT],
                        in_=hidst[hsl][:]))(mi, ti, hsl), SHD[hsl], SV)
                if ti == NT - 1:
                    wg_free[sl] = (SP, SP.n)
        esC1.close()
        self.esH3.close()

    def phase_c2(self):
        p, v, a, t, g, dma, S = self.p, self.v, self.a, self.t, self.g, self.dma, self.S
        nc = self.nc
        SV, SA, SP, SG, SEV = self.SV, self.SA, self.SP, self.SG, self.SEV
        T, NT = self.T, self.NT
        pb = self.pb
        self.gate()
        esC2 = contextlib.ExitStack()
        sbC = lambda name, shape, dt=F32: esC2.enter_context(nc.sbuf_tensor(name, shape, dt))
        wd = sbC("wd", [128, 64, 512], BF16)
        hidr = [sbC(f"hidr{i}", [128, 512], BF16) for i in range(8)]
        dstg = [sbC(f"dstg{i}", [128, 512]) for i in range(2)]
        LWD = S("lwd")
        LHD = [S(f"lhd{i}") for i in range(8)]
        SHD = [S(f"shd{i}") for i in range(3)]
        SDN = [S(f"sdn{i}") for i in range(2)]
        bank_free = {}
        hcons = {}
        fwd = 0
        for mgrp in range(4):
            p.wait("gpsimd", SP, fwd)
            for k in range(64):
                dma("gpsimd", (lambda mgrp, k: lambda e: e.dma_start(
                    out=wd[:, k, :],
                    in_=self.wdT[k * 128:(k + 1) * 128,
                                 mgrp * 512:(mgrp + 1) * 512]))(mgrp, k), LWD)
            for ti in range(NT):
                for k in range(64):
                    sl = k % 8
                    p.wait("sync", SP, hcons.get(sl, 0))
                    dma("sync", (lambda ti, k, sl: lambda e: e.dma_start(
                        out=hidr[sl][:],
                        in_=self.hid_d[k * 128:(k + 1) * 128, ti * WT:(ti + 1) * WT]))(ti, k, sl),
                        LHD[sl], SHD[0], SHD[1], SHD[2])
                    for m in range(4):
                        if k == 0:
                            self.bwait("tensor", bank_free.get(m))
                        t((lambda k, m, sl: lambda e: e.matmul(
                            pb[m][:, 0:512], wd[:, k, m * 128:(m + 1) * 128],
                            hidr[sl][:], start=(k == 0), stop=(k == 63)))(k, m, sl),
                          LHD[sl], *( [LWD] if k == 0 else [] ))
                    hcons[sl] = SP.n
                for m in range(4):
                    dsl = m % 2
                    p.wait("vector", SDN[dsl], SDN[dsl].n)
                    v((lambda m, dsl: lambda e: e.tensor_copy(
                        dstg[dsl][:], pb[m][:, 0:512]))(m, dsl), SP)
                    bank_free[m] = (SV, SV.n)
                    dma("sync", (lambda mgrp, m, ti, dsl: lambda e: e.dma_start(
                        out=self.down_d[(mgrp * 4 + m) * 128:(mgrp * 4 + m + 1) * 128,
                                        ti * WT:(ti + 1) * WT],
                        in_=dstg[dsl][:]))(mgrp, m, ti, dsl), SDN[dsl], SV)
            fwd = SP.n
        downt = sbC("downt", [128, KC, 512])
        xr2 = [sbC(f"xr2{i}", [128, 512]) for i in range(2)]
        outst = [sbC(f"outst{i}", [128, 512]) for i in range(2)]
        LDN, LX3 = S("ldn"), S("lx3")
        STO = [S(f"sto{i}") for i in range(2)]
        SX2 = S("sx2")
        dt_cons = 0
        xr_cons = [0, 0]
        for ti in range(NT):
            p.wait("sync", SV, dt_cons)
            for k in range(KC):
                dma("sync", (lambda ti, k: lambda e: e.dma_start(
                    out=downt[:, k, :],
                    in_=self.down_d[k * 128:(k + 1) * 128, ti * WT:(ti + 1) * WT]))(ti, k),
                    LDN, SDN[0], SDN[1])
            self.ln_stats(512, KC, (lambda ti: lambda k: downt[:, k, :])(ti), "ln",
                          src_deps=(LDN,))
            for k in range(KC):
                p.wait("sync", SV, xr_cons[k % 2])
                dma("sync", (lambda ti, k: lambda e: e.dma_start(
                    out=xr2[k % 2][:],
                    in_=self.x2_d[k * 128:(k + 1) * 128, ti * WT:(ti + 1) * WT]))(ti, k),
                    LX3, SX2)
                v((lambda k: lambda e: e.tensor_mul(self.lnt[:, 0:512], downt[:, k, :],
                                                    pb[4][:, 0:512]))(k), self.SP)
                v((lambda k: lambda e: e.tensor_add(self.lnt[:, 0:512], self.lnt[:, 0:512],
                                                    pb[5][:, 0:512]))(k))
                p.wait("vector", STO[k % 2], STO[k % 2].n)
                v((lambda k: lambda e: e.tensor_add(outst[k % 2][:], self.lnt[:, 0:512],
                                                    xr2[k % 2][:]))(k), LX3)
                xr_cons[k % 2] = SV.n
                dma("sync", (lambda ti, k: lambda e: e.dma_start(
                    out=self.out_d[k * 128:(k + 1) * 128, ti * WT:(ti + 1) * WT],
                    in_=outst[k % 2][:]))(ti, k), STO[k % 2], SV)
            dt_cons = SV.n
        p.wait("sync", STO[0], STO[0].n)
        p.wait("sync", STO[1], STO[1].n)
        esC2.close()

    def finalize(self):
        with self.nc.Block() as block:
            self.p.emit(block)
        self.es.close()
        return self.nc


def build(T=2048, debug=False):
    b = Builder(T=T, debug=debug)
    b.phase_consts()
    b.phase_a()
    b.phase_a2()
    b.phase_b()
    b.phase_b2()
    b.phase_c1()
    b.phase_c2()
    nc = b.finalize()
    return nc


def host_inputs(inputs, T=2048, n_cores=8):
    x = np.asarray(inputs["L0_attn_x"], np.float32)
    B, L, _ = x.shape
    npc = n_cores // B
    shared = {}
    shared["winT"] = np.ascontiguousarray(np.asarray(inputs["L0_attn_W_in"]).T)
    shared["woutT"] = np.ascontiguousarray(np.asarray(inputs["L0_attn_W_out"]).T)
    shared["wgT"] = np.ascontiguousarray(np.asarray(inputs["L0_ffn_w_gate"]).T)
    shared["wuT"] = np.ascontiguousarray(np.asarray(inputs["L0_ffn_w_up"]).T)
    shared["wdT"] = np.ascontiguousarray(np.asarray(inputs["L0_ffn_w_down"]).T)
    shared["rms_col"] = np.ascontiguousarray(
        np.asarray(inputs["L0_attn_rms_w"]).reshape(KC, 128).T)
    shared["norm_col"] = np.ascontiguousarray(
        np.asarray(inputs["L0_attn_norm_w"]).reshape(ZC, 128).T)
    shared["conv_w"] = np.ascontiguousarray(
        np.asarray(inputs["L0_attn_conv_w"]).reshape(XCC, 128, 4).transpose(1, 0, 2))
    shared["conv_b"] = np.ascontiguousarray(
        np.asarray(inputs["L0_attn_conv_b"]).reshape(XCC, 128).T)
    shared["dt_bias"] = np.asarray(inputs["L0_attn_dt_bias"]).reshape(64, 1)
    shared["a_log"] = np.asarray(inputs["L0_attn_A_log"]).reshape(64, 1)
    shared["d_rep"] = np.ascontiguousarray(
        np.repeat(np.asarray(inputs["L0_attn_D"]).reshape(ZC, 2).T, 64, axis=0))
    shared["idm"] = np.eye(128, dtype=np.float32)
    trl = np.zeros((128, 512), np.float32)
    for q in range(2):
        for i in range(128):
            trl[i, q * 256 + q * 128 + i:(q + 1) * 256] = 1.0
    shared["tril"] = trl.astype(bf16)
    xt_all = [np.ascontiguousarray(x[b].T) for b in range(B)]
    in_maps = []
    for core in range(n_cores):
        b, r = core // npc, core % npc
        m = dict(shared)
        sl = xt_all[b][:, r * T:(r + 1) * T]
        if r == 0:
            halo = np.zeros((DM, 3), np.float32)
        else:
            halo = xt_all[b][:, r * T - 3:r * T]
        m["x_fm"] = np.ascontiguousarray(np.concatenate([halo, sl], axis=1))
        mvv = np.zeros((128, 3), np.float32)
        muu = np.zeros((1, 3), np.float32)
        for j in range(3):
            if j < r:
                mvv[:, j] = 1.0
            if r > j:
                muu[0, j] = 1.0
        m["mask_v"] = mvv
        m["mask_u"] = muu
        m["mask_u1m"] = 1.0 - muu
        in_maps.append(m)
    return in_maps


def kernel(**inputs):
    from concourse.bass_utils import run_bass_kernel_spmd
    T = 2048
    n_cores = 8
    nc = build(T=T)
    in_maps = host_inputs(inputs, T=T, n_cores=n_cores)
    res = run_bass_kernel_spmd(nc, in_maps, core_ids=list(range(n_cores)))
    x = np.asarray(inputs["L0_attn_x"])
    B, L, _ = x.shape
    npc = n_cores // B
    out = np.empty((B, L, DM), np.float32)
    for core in range(n_cores):
        b, r = core // npc, core % npc
        out[b, r * T:(r + 1) * T, :] = res.results[core]["out_d"].T
    return out


# version 04
# speedup vs baseline: 1.0553x; 1.0553x over previous
import contextlib
import numpy as np
import ml_dtypes
import concourse.bass as bass
import concourse.mybir as mybir

F32 = mybir.dt.float32
BF16 = mybir.dt.bfloat16
AF = mybir.ActivationFunctionType
ALU = mybir.AluOpType
bf16 = ml_dtypes.bfloat16

DM = 2048
PROJ = 8512
ZD = 4096
XBC = 4352
NH = 64
HP = 64
NS = 128
CS = 256
KC = DM // 128
ZC = ZD // 128
XCC = XBC // 128
HID = 8192
EPS = 1e-5
WT = 512


class SemT:
    __slots__ = ("h", "n", "name")

    def __init__(self, es, nc, name):
        self.h = es.enter_context(nc.semaphore(name))
        self.n = 0
        self.name = name


class Prog:
    ENG = ("tensor", "vector", "scalar", "gpsimd", "sync")

    def __init__(self):
        self.q = {e: [] for e in self.ENG}
        self.mark = {e: {} for e in self.ENG}

    def wait(self, eng, sem, val=None):
        vv = sem.n if val is None else val
        if vv <= 0 or self.mark[eng].get(sem.name, 0) >= vv:
            return
        self.mark[eng][sem.name] = vv
        self.q[eng].append(("w", sem.h, vv))

    def op(self, eng, fn, incs=()):
        self.q[eng].append(("o", fn, list(incs)))
        for s, k in incs:
            s.n += k

    def emit(self, block):
        for name in self.ENG:
            fns = self.q[name]
            if not fns:
                continue

            def body(eng, fns=fns):
                for rec in fns:
                    if rec[0] == "w":
                        eng.wait_ge(rec[1], rec[2])
                    else:
                        ins = rec[1](eng)
                        for s, k in rec[2]:
                            ins.then_inc(s.h, k)
            getattr(block, name)(body)


class Builder:
    def __init__(self, T=2048, debug=False, grp=4):
        self.T = T
        self.NT = T // WT
        self.NCH = T // CS
        self.debug = debug
        self.grp = grp
        self.nc = bass.Bass("TRN2", target_bir_lowering=False)
        self.es = contextlib.ExitStack()
        self.sems = {}
        self.p = Prog()
        self._mk_dram()
        self._mk_sbuf()
        self.SV = self.S("sv")
        self.SA = self.S("sa")
        self.SP = self.S("sp")
        self.SG = self.S("sg")
        self.SEV = self.S("sev")

    def S(self, name):
        if name not in self.sems:
            self.sems[name] = SemT(self.es, self.nc, name)
        return self.sems[name]

    def sb(self, name, shape, dt=F32):
        return self.es.enter_context(self.nc.sbuf_tensor(name, shape, dt))

    def v(self, fn, *deps, ev=False):
        self.p.wait("vector", self.SV)
        for d in deps:
            self.p.wait("vector", d)
        self.p.op("vector", fn, incs=[(self.SV, 1)])

    def a(self, fn, *deps, ev=False):
        self.p.wait("scalar", self.SA)
        self.p.wait("scalar", self.SV, getattr(self, "const_done", 0))
        for d in deps:
            self.p.wait("scalar", d)
        self.p.op("scalar", fn, incs=[(self.SA, 1)])

    def bwait(self, eng, rec):
        if rec:
            self.p.wait(eng, rec[0], rec[1])

    def gate(self):
        for eng in Prog.ENG:
            for sem in list(self.sems.values()):
                self.p.wait(eng, sem)

    def t(self, fn, *deps, sp=True, extra=()):
        for d in deps:
            self.p.wait("tensor", d)
        self.p.op("tensor", fn, incs=[(self.SP, 1)])

    def g(self, fn, *deps):
        self.p.wait("gpsimd", self.SG)
        for d in deps:
            self.p.wait("gpsimd", d)
        self.p.op("gpsimd", fn, incs=[(self.SG, 1)])

    def dma(self, eng, fn, sem, *deps, extra=()):
        for d in deps:
            self.p.wait(eng, d)
        self.p.op(eng, fn, incs=[(sem, 16)] + list(extra))

    def _mk_dram(self):
        nc = self.nc
        T = self.T
        dk = "ExternalOutput" if self.debug else "Internal"

        def I_(name, shape, dtype=F32):
            return nc.dram_tensor(name, shape, dtype, kind="ExternalInput")
        self.x_fm = I_("x_fm", [DM, T + 3])
        self.winT = I_("winT", [DM, PROJ])
        self.woutT = I_("woutT", [ZD, DM])
        self.wgT = I_("wgT", [DM, HID])
        self.wuT = I_("wuT", [DM, HID])
        self.wdT = I_("wdT", [HID, DM])
        self.rms_col = I_("rms_col", [128, KC])
        self.norm_col = I_("norm_col", [128, ZC])
        self.conv_w = I_("conv_w", [128, XCC, 4])
        self.conv_b = I_("conv_b", [128, XCC])
        self.dt_bias = I_("dt_bias", [64, 1])
        self.a_log = I_("a_log", [64, 1])
        self.d_rep = I_("d_rep", [128, ZC])
        self.idm = I_("idm", [128, 128])
        self.tril = I_("tril", [128, 512], BF16)
        self.mask_v = I_("mask_v", [128, 3])
        self.mask_u = I_("mask_u", [1, 3])
        self.mask_u1m = I_("mask_u1m", [1, 3])
        self.ones16k = I_("ones16k", [1, 2048])

        self.out_d = nc.dram_tensor("out_d", [DM, T], F32, kind="ExternalOutput")
        self.z_d = nc.dram_tensor("z_d", [ZD, T], F32, kind=dk)
        self.xbc_d = nc.dram_tensor("xbc_d", [XBC, T + 3], F32, kind=dk)
        self.xcbf_d = nc.dram_tensor("xcbf_d", [XBC, T], BF16, kind=dk)
        self.xsf_d = nc.dram_tensor("xsf_d", [ZD, T], F32, kind=dk)
        self.bbf_d = nc.dram_tensor("bbf_d", [128, T], BF16, kind=dk)
        self.dacr_d = nc.dram_tensor("dacr_d", [64, T], F32, kind=dk)
        self.ndacr_d = nc.dram_tensor("ndacr_d", [64, T], F32, kind="Internal")
        self.erow_d = nc.dram_tensor("erow_d", [64, T], BF16, kind=dk)
        self.yn_d = nc.dram_tensor("yn_d", [ZD, T], BF16, kind=dk)
        self.x2_d = nc.dram_tensor("x2_d", [DM, T], F32, kind=dk)
        self.rowm_d = nc.dram_tensor("rowm_d", [1, T], F32, kind=dk)
        self.rowr_d = nc.dram_tensor("rowr_d", [1, T], F32, kind=dk)
        self.hid_d = nc.dram_tensor("hid_d", [HID, T], BF16, kind=dk)
        self.down_d = nc.dram_tensor("down_d", [DM, T], F32, kind=dk)
        self.cc_in = nc.dram_tensor("cc_in", [128, NH * HP], BF16)
        self.cc_out = nc.dram_tensor("cc_out", [self.grp * 128, NH * HP], BF16)
        self.ccd_in = nc.dram_tensor("ccd_in", [1, 64], F32)
        self.ccd_out = nc.dram_tensor("ccd_out", [self.grp, 64], F32)
        if self.debug:
            self.carry_dbg = nc.dram_tensor("carry_dbg", [128, NH * HP], F32,
                                            kind="ExternalOutput")
            self.sin_dbg = nc.dram_tensor("sin_dbg", [128, NH * HP], F32,
                                          kind="ExternalOutput")

    def _mk_sbuf(self):
        sb = self.sb
        T = self.T
        self.pb = [self.es.enter_context(self.nc.psum_tensor(f"pb{i}", [128, 512], F32))
                   for i in range(6)]
        self.ph = self.es.enter_context(self.nc.psum_tensor("ph", [3, 512], F32))
        self.pstat = self.es.enter_context(self.nc.psum_tensor("pstat", [1, 512], F32))

        self.ones_col = sb("ones_col", [128, 1])
        self.ones_row = sb("ones_row", [1, 128])
        self.ones_row_bf = sb("ones_row_bf", [33, 128], BF16)
        self.ident = sb("ident", [128, 128])
        self.rms_c = sb("rms_c", [128, KC])
        self.norm_c = sb("norm_c", [128, ZC])
        self.cw = sb("cw", [128, XCC, 4])
        self.cb = sb("cb", [128, XCC])
        self.dtb = sb("dtb", [64, 1])
        self.negA = sb("negA", [64, 1])
        self.drep = sb("drep", [128, ZC])
        self.trilm = sb("trilm", [128, 512], BF16)
        self.mv = sb("mv", [128, 3])
        self.mu = sb("mu", [1, 3])
        self.mu1m = sb("mu1m", [1, 3])
        self.z64 = sb("z64", [64, CS])
        self.cst = sb("cst", [128, 8])
        self.rowbuf = sb("rowbuf", [1, 6 * WT])
        self.sumx = sb("sumx", [128, WT])
        self.sumx2 = sb("sumx2", [128, WT])
        self.sqst = sb("sqst", [128, 2, WT])
        self.lnt = sb("lnt", [128, WT])
        self.esM = contextlib.ExitStack()
        sm = lambda n, shape, dt=F32: self.esM.enter_context(self.nc.sbuf_tensor(n, shape, dt))
        self.dac_tm = sm("dac_tm", [128, self.NCH, 2, 64])
        self.dt_tm = sm("dt_tm", [128, self.NCH, 2, 64])
        self.g_tm = sm("g_tm", [128, self.NCH, 2, 64])
        self.dec_b = sm("dec_b", [128, self.NCH, 64])
        self.b_bf = sm("b_bf", [128, T], BF16)
        self.c_bf = sm("c_bf", [128, T], BF16)
        self.carry = sm("carry", [128, NH * HP])
        self.esDtr = contextlib.ExitStack()
        self.dtr = self.esDtr.enter_context(self.nc.sbuf_tensor("dtr", [64, T], F32))

    def phase_consts(self):
        CLD = self.S("c_ld")
        pairs = [(self.ident, self.idm), (self.rms_c, self.rms_col),
                 (self.norm_c, self.norm_col), (self.cw, self.conv_w),
                 (self.cb, self.conv_b), (self.dtb, self.dt_bias),
                 (self.negA, self.a_log), (self.drep, self.d_rep),
                 (self.trilm, self.tril), (self.mv, self.mask_v),
                 (self.mu, self.mask_u), (self.mu1m, self.mask_u1m)]
        for dst, src in pairs:
            self.dma("sync", (lambda dst, src: lambda e: e.dma_start(
                out=dst[:], in_=src.ap()))(dst, src), CLD)
        self.v(lambda e: e.memset(self.ones_col[:], 1.0))
        self.v(lambda e: e.memset(self.ones_row[:], 1.0))
        self.v(lambda e: e.memset(self.ones_row_bf[0:1, :], 1.0))
        self.v(lambda e: e.memset(self.ones_row_bf[32:33, :], 1.0))
        self.v(lambda e: e.memset(self.z64[:], 0.0))
        self.v(lambda e: e.memset(self.carry[:], 0.0))
        constants = [1e-5, 1.0, -1.0, -0.5, 1.0 / 2048, 1.0 / 4096]
        for i, val in enumerate(constants):
            self.v((lambda i, val: lambda e: e.memset(self.cst[:, i:i + 1], val))(i, val))
            self.nc.const_aps.register(F32, val, self.cst[:, i:i + 1])
        self.const_done = self.SV.n
        self.a(lambda e: e.activation(self.negA[:], self.negA[:], AF.Exp), CLD)
        self.v(lambda e: e.tensor_single_scalar(self.negA[:], self.negA[:], -1.0, ALU.mult),
               self.SA)
        self.CLD = CLD

    def ln_stats(self, W, nsrc, src, kind, src_deps=()):
        p, v, a, t = self.p, self.v, self.a, self.t
        rb, pstat = self.rowbuf, self.pstat
        SV, SA, SP, SEV = self.SV, self.SA, self.SP, self.SEV
        for k in range(nsrc):
            a(lambda e, k=k: e.activation(self.sqst[:, k % 2, 0:W], src(k), AF.Square),
              *(tuple(src_deps) + (SV,)))
            if k == 0:
                v(lambda e: e.tensor_copy(self.sumx[:, 0:W], src(0)), *src_deps)
                v(lambda e: e.tensor_copy(self.sumx2[:, 0:W], self.sqst[:, 0, 0:W]), SA)
            else:
                v(lambda e, k=k: e.tensor_add(self.sumx[:, 0:W], self.sumx[:, 0:W], src(k)))
                v(lambda e, k=k: e.tensor_add(self.sumx2[:, 0:W], self.sumx2[:, 0:W],
                                              self.sqst[:, k % 2, 0:W]), SA)
        sc = 1.0 / (128 * nsrc)
        t(lambda e: e.matmul(pstat[0:1, 0:W], self.ones_col[:], self.sumx[:, 0:W],
                             start=True, stop=True), SV, SA)
        a(lambda e: e.activation(rb[0:1, 0:W], pstat[0:1, 0:W], AF.Copy, scale=sc),
          SP)
        t(lambda e: e.matmul(pstat[0:1, 0:W], self.ones_col[:], self.sumx2[:, 0:W],
                             start=True, stop=True), SA, SV)
        a(lambda e: e.activation(rb[0:1, WT:WT + W], pstat[0:1, 0:W], AF.Copy, scale=sc),
          SP)
        a(lambda e: e.activation(rb[0:1, 2 * WT:2 * WT + W], rb[0:1, 0:W], AF.Square))
        v(lambda e: e.tensor_sub(rb[0:1, 2 * WT:2 * WT + W], rb[0:1, WT:WT + W],
                                 rb[0:1, 2 * WT:2 * WT + W]), SA)
        a(lambda e: e.activation(rb[0:1, 3 * WT:3 * WT + W], rb[0:1, 2 * WT:2 * WT + W],
                                 AF.Ln, bias=1e-5), SV)
        a(lambda e: e.activation(rb[0:1, 3 * WT:3 * WT + W], rb[0:1, 3 * WT:3 * WT + W],
                                 AF.Exp, scale=-0.5))
        if kind == "ln_rms":
            a(lambda e: e.activation(rb[0:1, 4 * WT:4 * WT + W], rb[0:1, 3 * WT:3 * WT + W],
                                     AF.Square))
            v(lambda e: e.tensor_mul(rb[0:1, 4 * WT:4 * WT + W], rb[0:1, 4 * WT:4 * WT + W],
                                     rb[0:1, 2 * WT:2 * WT + W]), SA)
            a(lambda e: e.activation(rb[0:1, 4 * WT:4 * WT + W], rb[0:1, 4 * WT:4 * WT + W],
                                     AF.Ln, bias=EPS), SV)
            a(lambda e: e.activation(rb[0:1, 4 * WT:4 * WT + W], rb[0:1, 4 * WT:4 * WT + W],
                                     AF.Exp, scale=-0.5))
            v(lambda e: e.tensor_mul(rb[0:1, 4 * WT:4 * WT + W], rb[0:1, 4 * WT:4 * WT + W],
                                     rb[0:1, 3 * WT:3 * WT + W]), SA)
            a_off = 4 * WT
        else:
            a_off = 3 * WT
        v(lambda e: e.scalar_tensor_tensor(rb[0:1, 5 * WT:5 * WT + W], rb[0:1, 0:W], -1.0,
                                           rb[0:1, a_off:a_off + W], ALU.mult, ALU.mult),
          SA)
        t(lambda e: e.matmul(self.pb[4][:, 0:W], self.ones_row[:], rb[0:1, a_off:a_off + W],
                             start=True, stop=True), SV, SA)
        t(lambda e: e.matmul(self.pb[5][:, 0:W], self.ones_row[:],
                             rb[0:1, 5 * WT:5 * WT + W], start=True, stop=True), SA, SV)

    def ln_apply(self, W, nsrc, src, dst, mulcol=None, src_deps=()):
        v = self.v
        for k in range(nsrc):
            v(lambda e, k=k: e.tensor_mul(self.lnt[:, 0:W], src(k), self.pb[4][:, 0:W]),
              *(tuple(src_deps) + (self.SP,)))
            if mulcol is not None:
                v(lambda e, k=k: e.tensor_add(self.lnt[:, 0:W], self.lnt[:, 0:W],
                                              self.pb[5][:, 0:W]))
                v(lambda e, k=k: e.tensor_single_scalar(dst(k), self.lnt[:, 0:W],
                                                        mulcol(k), ALU.mult))
            else:
                v(lambda e, k=k: e.tensor_add(dst(k), self.lnt[:, 0:W], self.pb[5][:, 0:W]))

    def phase_a(self):
        p, v, a, t, g, dma, S = self.p, self.v, self.a, self.t, self.g, self.dma, self.S
        nc, sb, es = self.nc, self.sb, self.es
        SV, SA, SP, SG, SEV, CLD = self.SV, self.SA, self.SP, self.SG, self.SEV, self.CLD
        T, NT, NCH = self.T, self.NT, self.NCH
        pb, ph, pstat = self.pb, self.ph, self.pstat
        esA = contextlib.ExitStack()
        sbA = lambda name, shape, dt=F32: esA.enter_context(nc.sbuf_tensor(name, shape, dt))

        LXH = S("lxh")
        h_halo = sbA("h_halo", [128, KC, 3], BF16)
        xh = sbA("xh", [128, KC, 3])
        for k in range(KC):
            dma("sync", (lambda k: lambda e: e.dma_start(
                out=xh[:, k, :], in_=self.x_fm[k * 128:(k + 1) * 128, 0:3]))(k), LXH)
        self.ln_stats(3, KC, lambda k: xh[:, k, :], "ln_rms", src_deps=(LXH,))
        self.ln_apply(3, KC, lambda k: xh[:, k, :], lambda k: h_halo[:, k, :],
                      mulcol=lambda k: self.rms_c[:, k:k + 1], src_deps=(CLD,))

        whs = [sbA(f"whs{i}", [128, 512], BF16) for i in range(4)]
        LWH = [S(f"lwh{i}") for i in range(4)]
        whs_free = {}
        hst = sbA("hst", [3, 512])
        xbst = sbA("xbst", [128, 3])
        STH = S("sth")
        for j in range(9):
            wj = 512 if j < 8 else 256
            for k in range(KC):
                idx = j * KC + k
                sl = idx % 4
                if idx >= 4:
                    self.bwait("gpsimd", whs_free.get(sl))
                dma("gpsimd", (lambda j, k, sl, wj: lambda e: e.dma_start(
                    out=whs[sl][:, 0:wj],
                    in_=self.winT[k * 128:(k + 1) * 128,
                                  4096 + j * 512:4096 + j * 512 + wj]))(j, k, sl, wj),
                    LWH[sl])
                deps = [LWH[sl]]
                if k == 0:
                    deps += [SV]
                t((lambda j, k, wj, sl: lambda e: e.matmul(
                    ph[0:3, 0:wj], h_halo[:, k, :], whs[sl][:, 0:wj],
                    start=(k == 0), stop=(k == KC - 1)))(j, k, wj, sl), *deps)
                whs_free[sl] = (SP, SP.n)
            v((lambda wj: lambda e: e.tensor_copy(hst[0:3, 0:wj], ph[0:3, 0:wj]))(wj),
              SP)
            for s2 in range(wj // 128):
                t((lambda s2: lambda e: e.transpose(
                    pb[5][:, 0:3], hst[0:3, s2 * 128:(s2 + 1) * 128],
                    self.ident[0:3, 0:3]))(s2), SV, SA, CLD)
                v(lambda e: e.tensor_copy(xbst[:], pb[5][:, 0:3]), SP, STH)
                dma("sync", (lambda j, s2: lambda e: e.dma_start(
                    out=self.xbc_d[j * 512 + s2 * 128:j * 512 + (s2 + 1) * 128, 0:3],
                    in_=xbst[:]))(j, s2), STH, SV)

        h_sb = sbA("h_sb", [128, KC, T], BF16)
        xt = [sbA(f"xt{i}", [128, KC, 256]) for i in range(2)]
        LXT = [S(f"lxt{i}") for i in range(2)]
        NLN = T // 256
        self._xt_done = [0, 0]
        for ti in range(NLN):
            sl = ti % 2
            if ti >= 2:
                p.wait("sync", SV, self._xt_done[sl])
            for k in range(KC):
                dma("sync", (lambda ti, k, sl: lambda e: e.dma_start(
                    out=xt[sl][:, k, :],
                    in_=self.x_fm[k * 128:(k + 1) * 128, 3 + ti * 256:3 + (ti + 1) * 256]))(ti, k, sl),
                    LXT[sl])
            self.ln_stats(256, KC, (lambda sl: lambda k: xt[sl][:, k, :])(sl), "ln_rms",
                          src_deps=(LXT[sl],))
            SL1 = S("sl1")
            dma("sync", (lambda ti: lambda e: e.dma_start(
                out=self.rowm_d[0:1, ti * 256:(ti + 1) * 256],
                in_=self.rowbuf[0:1, 0:256]))(ti), SL1, SA)
            dma("sync", (lambda ti: lambda e: e.dma_start(
                out=self.rowr_d[0:1, ti * 256:(ti + 1) * 256],
                in_=self.rowbuf[0:1, 3 * WT:3 * WT + 256]))(ti), SL1, SA)
            p.wait("scalar", SL1, SL1.n)
            p.wait("vector", SL1, SL1.n)
            self.ln_apply(256, KC, (lambda sl: lambda k: xt[sl][:, k, :])(sl),
                          (lambda ti: lambda k: h_sb[:, k, ti * 256:(ti + 1) * 256])(ti),
                          mulcol=lambda k: self.rms_c[:, k:k + 1])
            self._xt_done[sl] = SV.n
        self.h_done = SV.n

        wp = [sbA(f"wp{i}", [128, KC, 384], BF16) for i in range(2)]
        LWP = [S(f"lwp{i}") for i in range(2)]
        wp_free = {}
        zst = [sbA(f"zst{i}", [128, 512]) for i in range(2)]
        SZT = [S(f"szt{i}") for i in range(2)]
        xst = [sbA(f"xst{i}", [128, 512]) for i in range(2)]
        SXT = [S(f"sxt{i}") for i in range(2)]
        bank_free = {}
        MGS = [3] * 22 + [1]
        zu = [0, 0]
        xu = [0, 0]
        for mg in range(23):
            nm = MGS[mg]
            sl = mg % 2
            if mg >= 2:
                self.bwait("gpsimd", wp_free.get(sl))
            for k in range(KC):
                mw = nm * 128 if mg < 22 else 64
                dma("gpsimd", (lambda mg, k, sl, mw: lambda e: e.dma_start(
                    out=wp[sl][:, k, 0:mw],
                    in_=self.winT[k * 128:(k + 1) * 128, mg * 384:mg * 384 + mw]))(mg, k, sl, mw),
                    LWP[sl])
            for ti in range(NT):
                par = ti % 2
                for k in range(KC):
                    for m in range(nm):
                        mi = mg * 3 + m
                        bank = par * 3 + m
                        mw = 128 if mi < 66 else 64
                        deps = [LWP[sl]]
                        if k == 0:
                            self.bwait("tensor", bank_free.get(bank))
                            p.wait("tensor", SV, self.h_done)
                        t((lambda k, m, mi, mw, ti, sl, bank: lambda e: e.matmul(
                            pb[bank][0:mw, 0:512],
                            wp[sl][:, k, m * 128:m * 128 + mw],
                            h_sb[:, k, ti * WT:(ti + 1) * WT],
                            start=(k == 0), stop=(k == KC - 1)))(k, m, mi, mw, ti, sl, bank),
                          *deps)
                for m in range(nm):
                    mi = mg * 3 + m
                    bank = par * 3 + m
                    if mi < 32:
                        zsl = (zu[0]) % 2
                        zu[0] += 1
                        a((lambda bank, zsl: lambda e: e.activation(
                            zst[zsl][:], pb[bank][:, 0:512], AF.Silu))(bank, zsl),
                          SP, SZT[zsl])
                        dma("sync", (lambda mi, ti, zsl: lambda e: e.dma_start(
                            out=self.z_d[mi * 128:(mi + 1) * 128, ti * WT:(ti + 1) * WT],
                            in_=zst[zsl][:]))(mi, ti, zsl), SZT[zsl], SA)
                    elif mi < 66:
                        xsl = (xu[0]) % 2
                        xu[0] += 1
                        v((lambda bank, xsl: lambda e: e.tensor_copy(
                            xst[xsl][:], pb[bank][:, 0:512]))(bank, xsl),
                          SP, SXT[xsl])
                        dma("sync", (lambda mi, ti, xsl: lambda e: e.dma_start(
                            out=self.xbc_d[(mi - 32) * 128:(mi - 31) * 128,
                                           3 + ti * WT:3 + (ti + 1) * WT],
                            in_=xst[xsl][:]))(mi, ti, xsl), SXT[xsl], SV)
                    else:
                        v((lambda bank, ti: lambda e: e.tensor_copy(
                            self.dtr[:, ti * WT:(ti + 1) * WT], pb[bank][0:64, 0:512]))(bank, ti),
                          SP)
                    if mi < 32:
                        bank_free[bank] = (SA, SA.n)
                    else:
                        bank_free[bank] = (SV, SV.n)
                if ti == NT - 1:
                    wp_free[sl] = (SP, SP.n)
        self.esA1 = esA

    def _states_pass(self, c, xsT, gxsT, BT, LTR, LBT, bank_free):
        p, v, t, S = self.p, self.v, self.t, self.S
        pb = self.pb
        SV, SP, SEV = self.SV, self.SP, self.SEV
        for half in range(2):
            for q in range(2):
                v((lambda c, q, half: lambda e: e.tensor_tensor(
                    gxsT[:, q, :], xsT[:, q, half * 2048:(half + 1) * 2048],
                    bass.AP(self.g_tm, (c * 2 + q) * 64 + half * 32,
                            [[self.NCH * 2 * 64, 128], [1, 32], [0, 64]]),
                    ALU.mult))(c, q, half), LTR, SP)
            for q in range(2):
                for hb in range(4):
                    deps = [LBT, SV] if (q == 0 and hb == 0) else []
                    if q == 0:
                        self.bwait("tensor", bank_free.get(("st", hb)))
                    t((lambda half, q, hb: lambda e: e.matmul(
                        pb[hb][:, 0:512], BT[:, q, :],
                        gxsT[:, q, hb * 512:(hb + 1) * 512],
                        start=(q == 0), stop=(q == 1)))(half, q, hb),
                      *deps, sp=(q == 1))
            for hb in range(4):
                for h8 in range(8):
                    h = half * 32 + hb * 8 + h8
                    v((lambda c, h, hb, h8: lambda e: e.scalar_tensor_tensor(
                        self.carry[:, h * 64:(h + 1) * 64],
                        self.carry[:, h * 64:(h + 1) * 64],
                        self.dec_b[:, c, h:h + 1],
                        pb[hb][:, h8 * 64:(h8 + 1) * 64],
                        ALU.mult, ALU.add))(c, h, hb, h8),
                      *( [self.SP] if (hb == 0 and h8 == 0) else [] ))
                bank_free[("st", hb)] = (self.SV, self.SV.n)

    def phase_a2(self):
        p, v, a, t, g, dma, S = self.p, self.v, self.a, self.t, self.g, self.dma, self.S
        nc = self.nc
        SV, SA, SP, SG, SEV, CLD = self.SV, self.SA, self.SP, self.SG, self.SEV, self.CLD
        T, NT, NCH = self.T, self.NT, self.NCH
        pb, pstat = self.pb, self.pstat
        self.esA1.close()
        self.gate()
        esA2 = contextlib.ExitStack()
        sbA = lambda name, shape, dt=F32: esA2.enter_context(nc.sbuf_tensor(name, shape, dt))

        cvin = [sbA(f"cvin{i}", [128, 515]) for i in range(3)]
        LCV = [S(f"lcv{i}") for i in range(3)]
        cvac = [sbA(f"cvac{i}", [128, 512]) for i in range(2)]
        cvout = [sbA(f"cvout{i}", [128, 512]) for i in range(3)]
        SCB = S("scb")
        SCS = S("scs")
        SBB = S("sbb")
        cvu = {}
        SXT0, SXT1, STH = S("sxt0"), S("sxt1"), S("sth")
        for ti in range(NT):
            for mc in range(XCC):
                idx = ti * XCC + mc
                sl = idx % 3
                osl = idx % 3
                if idx >= 3:
                    p.wait("sync", SV, cvu.get(("in", sl), 0))
                dma("sync", (lambda ti, mc, sl: lambda e: e.dma_start(
                    out=cvin[sl][:],
                    in_=self.xbc_d[mc * 128:(mc + 1) * 128, ti * WT:ti * WT + 515]))(ti, mc, sl),
                    LCV[sl], SXT0, SXT1, STH)
                asl = idx % 2
                v((lambda mc, sl, asl: lambda e: e.tensor_scalar(
                    cvac[asl][:], cvin[sl][:, 0:512], self.cw[:, mc, 0:1], None,
                    ALU.mult))(mc, sl, asl), LCV[sl], CLD, SA)
                for kk in range(1, 4):
                    v((lambda mc, sl, asl, kk: lambda e: e.scalar_tensor_tensor(
                        cvac[asl][:], cvin[sl][:, kk:512 + kk], self.cw[:, mc, kk:kk + 1],
                        cvac[asl][:], ALU.mult, ALU.add))(mc, sl, asl, kk))
                cvu[("in", sl)] = SV.n
                p.wait("scalar", SCB)
                p.wait("scalar", SCS)
                p.wait("scalar", SBB)
                p.wait("scalar", SV)
                a((lambda mc, asl, osl: lambda e: e.activation(
                    cvout[osl][:], cvac[asl][:], AF.Silu, bias=self.cb[:, mc:mc + 1]))(mc, asl, osl),
                  SV)
                dma("gpsimd", (lambda ti, mc, osl: lambda e: e.dma_start(
                    out=self.xcbf_d[mc * 128:(mc + 1) * 128, ti * WT:(ti + 1) * WT],
                    in_=cvout[osl][:]))(ti, mc, osl), SCB, SA)
                if mc < 32:
                    dma("sync", (lambda ti, mc, osl: lambda e: e.dma_start(
                        out=self.xsf_d[mc * 128:(mc + 1) * 128, ti * WT:(ti + 1) * WT],
                        in_=cvout[osl][:]))(ti, mc, osl), SCS, SA)
                elif mc == 32:
                    v((lambda ti, osl: lambda e: e.tensor_copy(
                        self.b_bf[:, ti * WT:(ti + 1) * WT], cvout[osl][:]))(ti, osl), SA)
                    dma("gpsimd", (lambda ti, osl: lambda e: e.dma_start(
                        out=self.bbf_d[:, ti * WT:(ti + 1) * WT], in_=cvout[osl][:]))(ti, osl),
                        SBB, SA)
                else:
                    v((lambda ti, osl: lambda e: e.tensor_copy(
                        self.c_bf[:, ti * WT:(ti + 1) * WT], cvout[osl][:]))(ti, osl), SA)

        daw = sbA("daw", [64, T])
        av = sbA("av", [64, T])
        dts = self.dts = sbA("dts", [64, T])
        dac = self.dac = sbA("dac", [64, T])
        ehm = self.ehm = sbA("ehm", [64, T])
        ghm = self.ghm = sbA("ghm", [64, T])
        v(lambda e: e.tensor_scalar(dts[:], self.dtr[:], self.dtb[:, 0:1], None, ALU.add),
          CLD)
        v(lambda e: e.tensor_scalar(av[:], dts[:], -1.0, None, ALU.mult))
        v(lambda e: e.tensor_max(av[:], av[:], dts[:]))
        a(lambda e: e.activation(av[:], av[:], AF.Exp, scale=-1.0), SV)
        a(lambda e: e.activation(av[:], av[:], AF.Ln, bias=1.0))
        v(lambda e: e.tensor_scalar(dts[:], dts[:], 0.0, None, ALU.max))
        v(lambda e: e.tensor_add(dts[:], dts[:], av[:]), SA)
        v(lambda e: e.tensor_scalar(daw[:], dts[:], self.negA[:, 0:1], None, ALU.mult))
        for c in range(NCH):
            v((lambda c: lambda e: e.tensor_tensor_scan(
                dac[:, c * CS:(c + 1) * CS], daw[:, c * CS:(c + 1) * CS],
                self.z64[:], 0.0, ALU.add, ALU.add))(c))
        a(lambda e: e.activation(ehm[:], dac[:], AF.Exp), SV)
        ndac = sbA("ndac", [64, T])
        v(lambda e: e.tensor_single_scalar(ndac[:], dac[:], -1.0, ALU.mult))
        DMR = S("dmr")
        dma("sync", lambda e: e.dma_start(out=self.dacr_d.ap(), in_=dac[:]), DMR, SV)
        dma("sync", lambda e: e.dma_start(out=self.ndacr_d.ap(), in_=ndac[:]), DMR, SV)
        dma("gpsimd", lambda e: e.dma_start(out=self.erow_d.ap(), in_=ehm[:]), S("dmg"), SA)
        for c in range(NCH):
            a((lambda c: lambda e: e.activation(
                ghm[:, c * CS:(c + 1) * CS], dac[:, c * CS:(c + 1) * CS], AF.Exp,
                scale=-1.0, bias=dac[:, c * CS + CS - 1:c * CS + CS]))(c), SV)
        v(lambda e: e.tensor_mul(ghm[:], ghm[:], dts[:]), SA)
        decrow = sbA("decrow", [1, 64])
        for c in range(NCH):
            for (src, dst) in ((dac, self.dac_tm), (dts, self.dt_tm), (ghm, self.g_tm)):
                for q in range(2):
                    t((lambda src, c, q: lambda e: e.transpose(
                        pb[5][:, 0:64], src[:, c * CS + q * 128:c * CS + (q + 1) * 128],
                        self.ident[0:64, 0:64]))(src, c, q), SV, SA, CLD)
                    v((lambda dst, c, q: lambda e: e.tensor_copy(
                        dst[:, c, q, :], pb[5][:, 0:64]))(dst, c, q), SP)
            t((lambda c: lambda e: e.transpose(
                pstat[0:1, 0:64], ehm[:, c * CS + CS - 1:c * CS + CS],
                self.ident[0:64, 0:64]))(c), SV, SA)
            v(lambda e: e.tensor_copy(decrow[:], pstat[0:1, 0:64]), SP)
            t(lambda e: e.matmul(pb[5][:, 0:64], self.ones_row[:], decrow[:],
                                 start=True, stop=True), SV, SA)
            v((lambda c: lambda e: e.tensor_copy(self.dec_b[:, c, :], pb[5][:, 0:64]))(c),
              SP)
        dtot = sbA("dtot", [64, 1])
        dtotrow = sbA("dtotrow", [1, 64])
        v(lambda e: e.reduce_sum(dtot[:], bass.AP(dac, CS - 1,
            [[self.T, 64], [CS, NCH]]), axis=mybir.AxisListType.X), SV)
        a(lambda e: e.activation(dtot[:], dtot[:], AF.Exp), SV)
        t(lambda e: e.transpose(pstat[0:1, 0:64], dtot[:], self.ident[0:64, 0:64]),
          SA, SV)
        v(lambda e: e.tensor_copy(dtotrow[:], pstat[0:1, 0:64]), SP)
        SCC = S("scc")
        SCG = S("scg")
        dma("sync", lambda e: e.dma_start(out=self.ccd_in.ap(), in_=dtotrow[:]), SCC, SV)

        xsT = sbA("xsT", [128, 2, ZD], BF16)
        gxsT = sbA("gxsT", [128, 2, 2048], BF16)
        BT = sbA("BT", [128, 2, 128], BF16)
        LTR = S("ltr")
        LBT = S("lbt")
        bank_free = {}
        for c in range(NCH):
            for cc in range(32):
                for q in range(2):
                    dma("sync", (lambda c, cc, q: lambda e: e.dma_start(
                        out=xsT[:, q, cc * 128:(cc + 1) * 128],
                        in_=self.xcbf_d[cc * 128:(cc + 1) * 128,
                                        c * CS + q * 128:c * CS + (q + 1) * 128],
                        transpose=True))(c, cc, q), LTR, SCB, SG)
            for q in range(2):
                dma("sync", (lambda c, q: lambda e: e.dma_start(
                    out=BT[:, q, :],
                    in_=self.bbf_d[:, c * CS + q * 128:c * CS + (q + 1) * 128],
                    transpose=True))(c, q), LBT, SBB, SG)
            self._states_pass(c, xsT, gxsT, BT, LTR, LBT, bank_free)
            p.wait("sync", SP)

        if self.debug:
            DBG = S("dbg")
            dma("sync", lambda e: e.dma_start(out=self.carry_dbg.ap(), in_=self.carry[:]),
                DBG, SV)

        carrybf = sbA("carrybf", [128, NH * HP], BF16)
        v(lambda e: e.tensor_copy(carrybf[:], self.carry[:]))
        dma("gpsimd", lambda e: e.dma_start(out=self.cc_in.ap(), in_=carrybf[:]), SCG, SV)
        SCCD = S("sccd")
        groups = [[0, 1, 2, 3], [4, 5, 6, 7]] if self.grp == 4 else [list(range(8))]
        p.wait("gpsimd", SCC, SCC.n)
        p.wait("gpsimd", SCG, SCG.n)
        p.op("gpsimd", (lambda groups: lambda e: e.collective_compute(
            "AllGather", ALU.bypass, replica_groups=groups,
            ins=[self.cc_in.ap().opt()],
            outs=[self.cc_out.ap().opt()]))(groups),
            incs=[(SCCD, 1)])
        p.op("gpsimd", (lambda groups: lambda e: e.collective_compute(
            "AllGather", ALU.bypass, replica_groups=groups,
            ins=[self.ccd_in.ap().opt()],
            outs=[self.ccd_out.ap().opt()]))(groups),
            incs=[(SCCD, 1)])

        slotb = sbA("slotb", [128, NH * HP], BF16)
        scr = sbA("scr", [128, NH * HP])
        dcol = sbA("dcol", [1, 64])
        dtbc = sbA("dtbc", [128, 2, 64])
        LSL = S("lsl")
        p.wait("sync", SCCD, SCCD.n)
        dma("sync", lambda e: e.dma_start(out=slotb[:], in_=self.cc_out[0:128, :]), LSL)
        v(lambda e: e.tensor_scalar(self.carry[:], slotb[:], self.mv[:, 0:1], None,
                                    ALU.mult), LSL, S("dbg"))
        for j in (1, 2):
            dma("sync", (lambda j: lambda e: e.dma_start(
                out=dcol[:], in_=self.ccd_out[j:j + 1, :]))(j), LSL, SV)
            v((lambda j: lambda e: e.scalar_tensor_tensor(
                dcol[:], dcol[:], self.mu[0:1, j:j + 1],
                bass.AP(self.mu1m, j, [[3, 1], [0, 64]]), ALU.mult, ALU.add))(j), LSL)
            t(lambda e: e.matmul(pb[4][:, 0:64], self.ones_row[:], dcol[:],
                                 start=True, stop=True), SV, SA)
            v((lambda j: lambda e: e.tensor_copy(dtbc[:, j - 1, :], pb[4][:, 0:64]))(j),
              SP)
            dma("sync", (lambda j: lambda e: e.dma_start(
                out=slotb[:], in_=self.cc_out[j * 128:(j + 1) * 128, :]))(j), LSL, SV)
            v((lambda j: lambda e: e.tensor_scalar(
                scr[:], slotb[:], self.mv[:, j:j + 1], None, ALU.mult))(j), LSL)
            v((lambda j: lambda e: e.tensor_tensor(
                self.carry[:], self.carry[:],
                bass.AP(dtbc, (j - 1) * 64, [[2 * 64, 128], [1, 64], [0, 64]]),
                ALU.mult))(j))
            v(lambda e: e.tensor_add(self.carry[:], self.carry[:], scr[:]))
        if self.debug:
            DBG = S("dbg")
            dma("sync", lambda e: e.dma_start(out=self.sin_dbg.ap(), in_=self.carry[:]),
                DBG, SV)
        self.esA2 = esA2

    def phase_b(self):
        p, v, a, t, g, dma, S = self.p, self.v, self.a, self.t, self.g, self.dma, self.S
        nc = self.nc
        SV, SA, SP, SG, SEV, CLD = self.SV, self.SA, self.SP, self.SG, self.SEV, self.CLD
        T, NT, NCH = self.T, self.NT, self.NCH
        pb, pstat = self.pb, self.pstat
        self.esA2.close()
        self.esDtr.close()
        self.gate()
        esB = contextlib.ExitStack()
        sbB = lambda name, shape, dt=F32: esB.enter_context(nc.sbuf_tensor(name, shape, dt))

        z_sb = sbB("z_sb", [128, 16, 256])
        xs_sb = sbB("xs_sb", [128, 16, 256])
        y_sb = sbB("y_sb", [128, 32, 256])
        xsT = sbB("xsTb", [128, 2, ZD], BF16)
        gxsT = sbB("gxsTb", [128, 2, 2048], BF16)
        BT = sbB("BTb", [128, 2, 128], BF16)
        ps_bf = sbB("ps_bf", [128, NH * HP], BF16)
        ynv = lambda cc: bass.AP(xsT, cc * 256, [[2 * ZD, 128], [1, 256]])
        NSL = 4
        rl = [sbB(f"rlb{i}", [128, 512]) for i in range(2)]
        Lb = [sbB(f"Lbb{i}", [128, 512], BF16) for i in range(NSL)]
        MT = [sbB(f"MTb{i}", [128, 512], BF16) for i in range(NSL)]
        Ce = [sbB(f"Ceb{i}", [128, 256], BF16) for i in range(NSL)]
        segpackL = sbB("segpackL", [34, 2048])
        segpackR = sbB("segpackR", [34, 2048])
        erwpack = sbB("erwpack", [33, 2048], BF16)
        sgl = [segpackL[0:2, :], segpackL[32:34, :]]
        sgr = [segpackR[0:2, :], segpackR[32:34, :]]
        erw = [erwpack[0:1, :], erwpack[32:33, :]]
        CBt = sbB("CBtb", [128, 512], BF16)
        sqstB = rl[0]
        sumxB = self.lnt
        rowB = self.rowbuf

        LZC, LXS, LTR2, LBT2 = S("lzc"), S("lxs"), S("ltr2"), S("lbt2")
        LSG = [S(f"lsg{i}") for i in range(2)]
        LER = [S(f"ler{i}") for i in range(2)]
        SYN = S("syn")
        DMR = S("dmr")
        DMG = S("dmg")
        SCB = S("scb")
        SBB = S("sbb")
        SZT = [S("szt0"), S("szt1")]
        SCS = S("scs")
        bank_free = {}
        econs = {}
        eb_free = {}
        sg_cons = {}
        seg_free = {}
        lb_free = {}
        mt_free = {}
        ce_free = {}
        y_free = {}
        zy_done = 0
        xs_done = 0
        yn_done = 0

        LON = S("lon")
        for i in range(2):
            dma("sync", (lambda i: lambda e: e.dma_start(
                out=sgl[i][1:2, :], in_=self.ones16k.ap()))(i), LON)
            dma("sync", (lambda i: lambda e: e.dma_start(
                out=sgr[i][0:1, :], in_=self.ones16k.ap()))(i), LON)

        for c in range(NCH):
            for cc in range(16):
                p.wait("sync", SV, zy_done)
                dma("sync", (lambda c, cc: lambda e: e.dma_start(
                    out=z_sb[:, cc, :],
                    in_=self.z_d[cc * 128:(cc + 1) * 128, c * CS:(c + 1) * CS]))(c, cc),
                    LZC, SZT[0], SZT[1])
            for cc in range(16):
                p.wait("sync", SV, xs_done)
                dma("sync", (lambda c, cc: lambda e: e.dma_start(
                    out=xs_sb[:, cc, :],
                    in_=self.xsf_d[cc * 128:(cc + 1) * 128, c * CS:(c + 1) * CS]))(c, cc),
                    LXS, SCS)
            for cc in range(32):
                for q in range(2):
                    dma("sync", (lambda c, cc, q: lambda e: e.dma_start(
                        out=xsT[:, q, cc * 128:(cc + 1) * 128],
                        in_=self.xcbf_d[cc * 128:(cc + 1) * 128,
                                        c * CS + q * 128:c * CS + (q + 1) * 128],
                        transpose=True))(c, cc, q), LTR2, SCB, SP, SYN)
            for q in range(2):
                dma("sync", (lambda c, q: lambda e: e.dma_start(
                    out=BT[:, q, :],
                    in_=self.bbf_d[:, c * CS + q * 128:c * CS + (q + 1) * 128],
                    transpose=True))(c, q), LBT2, SBB, SP)
            v(lambda e: e.tensor_copy(ps_bf[:], self.carry[:]), SP)
            self.bwait("tensor", bank_free.get(4))
            for q in range(2):
                t((lambda c, q: lambda e: e.matmul(
                    pb[4][:, q * 256:(q + 1) * 256],
                    self.b_bf[:, c * CS + q * 128:c * CS + (q + 1) * 128],
                    self.c_bf[:, c * CS:(c + 1) * CS],
                    start=True, stop=True))(c, q))
            v(lambda e: e.tensor_tensor(CBt[:], pb[4][:, 0:512], self.trilm[:], ALU.mult),
              SP)
            bank_free[4] = (SV, SV.n)
            for h in range(NH):
                s2 = h % 2
                s4 = h % NSL
                h8 = h % 8
                g8 = (h // 8) % 2
                if h8 == 0:
                    self.bwait("sync", sg_cons.get(("g", g8)))
                    dma("sync", (lambda c, h, g8: lambda e: e.dma_start(
                        out=sgl[g8][0:1, :],
                        in_=bass.AP(self.ndacr_d, h * T + c * CS,
                                    [[1, 1], [T, 8], [1, 256]])))(c, h, g8),
                        LSG[g8], DMR, LON)
                    dma("sync", (lambda c, h, g8: lambda e: e.dma_start(
                        out=sgr[g8][1:2, :],
                        in_=bass.AP(self.dacr_d, h * T + c * CS,
                                    [[1, 1], [T, 8], [1, 256]])))(c, h, g8), LSG[g8])
                    dma("sync", (lambda c, h, g8: lambda e: e.dma_start(
                        out=erw[g8][:],
                        in_=bass.AP(self.erow_d, h * T + c * CS,
                                    [[1, 1], [T, 8], [1, 256]])))(c, h, g8),
                        LER[g8], DMG)
                self.bwait("tensor", seg_free.get(s2))
                for q in range(2):
                    t((lambda h, h8, s2, g8, q: lambda e: e.matmul(
                        pb[s2][:, q * 256:(q + 1) * 256],
                        sgl[g8][:, h8 * 256 + q * 128:h8 * 256 + (q + 1) * 128],
                        sgr[g8][:, h8 * 256:(h8 + 1) * 256],
                        start=True, stop=True))(h, h8, s2, g8, q),
                      *( [LSG[g8]] if q == 0 else [] ))
                self.bwait("tensor", eb_free.get(s2))
                t((lambda h, h8, s2, g8: lambda e: e.matmul(
                    pb[5][:, s2 * 256:(s2 + 1) * 256],
                    self.ones_row_bf[g8 * 32:g8 * 32 + 1, :],
                    erw[g8][0:1, h8 * 256:(h8 + 1) * 256],
                    start=True, stop=True))(h, h8, s2, g8), LER[g8])
                a((lambda s2: lambda e: e.activation(rl[s2][:], pb[s2][:, 0:512],
                                                     AF.Relu, scale=-1.0))(s2), SP)
                seg_free[s2] = (SA, SA.n)
                self.bwait("scalar", lb_free.get(s4))
                a((lambda s2, s4: lambda e: e.activation(Lb[s4][:], rl[s2][:],
                                                         AF.Exp, scale=-1.0))(s2, s4))
                self.bwait("vector", mt_free.get(s4))
                for q in range(2):
                    v((lambda c, h, s4, q: lambda e: e.scalar_tensor_tensor(
                        MT[s4][:, q * 256:(q + 1) * 256],
                        Lb[s4][:, q * 256:(q + 1) * 256],
                        self.dt_tm[:, c, q, h:h + 1],
                        CBt[:, q * 256:(q + 1) * 256],
                        ALU.mult, ALU.mult))(c, h, s4, q), SA)
                lb_free[s4] = (SV, SV.n)
                self.bwait("vector", ce_free.get(s4))
                v((lambda c, s2, s4: lambda e: e.tensor_tensor(
                    Ce[s4][:], self.c_bf[:, c * CS:(c + 1) * CS],
                    pb[5][:, s2 * 256:(s2 + 1) * 256], ALU.mult))(c, s2, s4), SP)
                eb_free[s2] = (SV, SV.n)
                yreg = h % 4
                yb = 2 + yreg // 2
                yo = (yreg % 2) * 256
                self.bwait("tensor", y_free.get(yreg))
                t((lambda h, s4, yb, yo: lambda e: e.matmul(
                    pb[yb][0:64, yo:yo + 256], xsT[:, 0, h * 64:(h + 1) * 64],
                    MT[s4][:, 0:256], start=True, stop=False))(h, s4, yb, yo),
                  LTR2, SV)
                t((lambda h, s4, yb, yo: lambda e: e.matmul(
                    pb[yb][0:64, yo:yo + 256], xsT[:, 1, h * 64:(h + 1) * 64],
                    MT[s4][:, 256:512], start=False, stop=False))(h, s4, yb, yo))
                t((lambda h, s4, yb, yo: lambda e: e.matmul(
                    pb[yb][0:64, yo:yo + 256], ps_bf[:, h * 64:(h + 1) * 64],
                    Ce[s4][:], start=False, stop=True))(h, s4, yb, yo))
                if h8 == 7:
                    sg_cons[("g", g8)] = (SP, SP.n)
                mt_free[s4] = (SP, SP.n)
                ce_free[s4] = (SP, SP.n)
                base = 64 * (h % 2)
                cc = h // 2
                v((lambda h, base, cc, yb, yo: lambda e: e.scalar_tensor_tensor(
                    y_sb[base:base + 64, cc, :], xs_sb[base:base + 64, cc % 16, :],
                    self.drep[base:base + 64, cc:cc + 1], pb[yb][0:64, yo:yo + 256],
                    ALU.mult, ALU.add))(h, base, cc, yb, yo), SP, LXS)
                y_free[yreg] = (SV, SV.n)
                if h == 31:
                    for cc2 in range(16, 32):
                        p.wait("sync", SV, SV.n)
                        dma("sync", (lambda c, cc2: lambda e: e.dma_start(
                            out=xs_sb[:, cc2 - 16, :],
                            in_=self.xsf_d[cc2 * 128:(cc2 + 1) * 128,
                                           c * CS:(c + 1) * CS]))(c, cc2), LXS)
            xs_done = SV.n
            for hb in (0, 1):
                self.bwait("tensor", seg_free.get(hb))
            for yreg in range(4):
                self.bwait("tensor", y_free.get(yreg))
            self._states_pass(c, xsT, gxsT, BT, LTR2, LBT2, bank_free)
            p.wait("sync", SP)
            for cc in range(32):
                if cc == 16:
                    for cc2 in range(16, 32):
                        p.wait("sync", SV, SV.n)
                        dma("sync", (lambda c, cc2: lambda e: e.dma_start(
                            out=z_sb[:, cc2 - 16, :],
                            in_=self.z_d[cc2 * 128:(cc2 + 1) * 128,
                                         c * CS:(c + 1) * CS]))(c, cc2), LZC)
                v((lambda cc: lambda e: e.tensor_mul(
                    y_sb[:, cc, :], y_sb[:, cc, :], z_sb[:, cc % 16, :]))(cc),
                  *( [LZC] if cc in (0, 16) else [] ))
            zy_done = SV.n
            for cc in range(32):
                a((lambda cc: lambda e: e.activation(
                    sqstB[:, (cc % 2) * 256:(cc % 2) * 256 + 256],
                    y_sb[:, cc, :], AF.Square))(cc), SV)
                if cc == 0:
                    v(lambda e: e.tensor_copy(sumxB[:, 0:256], sqstB[:, 0:256]), SA)
                else:
                    v((lambda cc: lambda e: e.tensor_add(
                        sumxB[:, 0:256], sumxB[:, 0:256],
                        sqstB[:, (cc % 2) * 256:(cc % 2) * 256 + 256]))(cc), SA)
            t(lambda e: e.matmul(pstat[0:1, 0:256], self.ones_col[:], sumxB[:, 0:256],
                                 start=True, stop=True), SV, SA)
            a(lambda e: e.activation(rowB[0:1, 0:256], pstat[0:1, 0:256], AF.Copy,
                                     scale=1.0 / ZD), SP)
            a(lambda e: e.activation(rowB[0:1, 0:256], rowB[0:1, 0:256], AF.Ln, bias=EPS))
            a(lambda e: e.activation(rowB[0:1, 0:256], rowB[0:1, 0:256], AF.Exp, scale=-0.5))
            self.bwait("tensor", bank_free.get(4))
            t(lambda e: e.matmul(pb[4][:, 0:256], self.ones_row[:],
                                 rowB[0:1, 0:256], start=True, stop=True), SA, SV)
            for cc in range(32):
                p.wait("vector", SYN, yn_done)
                v((lambda cc: lambda e: e.scalar_tensor_tensor(
                    ynv(cc), y_sb[:, cc, :], self.norm_c[:, cc:cc + 1],
                    pb[4][:, 0:256], ALU.mult, ALU.mult))(cc), SP)
            bank_free[4] = (SV, SV.n)
            for cc in range(32):
                dma("sync", (lambda c, cc: lambda e: e.dma_start(
                    out=self.yn_d[cc * 128:(cc + 1) * 128, c * CS:(c + 1) * CS],
                    in_=ynv(cc)))(c, cc), SYN, SV)
            yn_done = SYN.n
        self.esB = esB

    def phase_b2(self):
        p, v, a, t, g, dma, S = self.p, self.v, self.a, self.t, self.g, self.dma, self.S
        nc = self.nc
        SV, SA, SP, SG, SEV = self.SV, self.SA, self.SP, self.SG, self.SEV
        T, NT = self.T, self.NT
        pb = self.pb
        self.esB.close()
        self.esM.close()
        self.gate()
        self.esH3 = contextlib.ExitStack()
        h3 = self.h3 = self.esH3.enter_context(nc.sbuf_tensor("h3", [128, KC, T], BF16))
        esB2 = contextlib.ExitStack()
        sbB = lambda name, shape, dt=F32: esB2.enter_context(nc.sbuf_tensor(name, shape, dt))
        wo = sbB("wo", [128, 32, 512], BF16)
        yn_t = sbB("yn_t", [128, 32, 512], BF16)
        attn = sbB("attn", [128, 16, 512])
        xr = [sbB(f"xr{i}", [128, 512]) for i in range(2)]
        LWO, LYN, LX2, SX2 = S("lwo"), S("lyn"), S("lx2"), S("sx2")
        SYN = S("syn")
        bank_free = {}
        fwo = 0
        lyn_cons = 0
        for ti in range(NT):
            p.wait("sync", SP, lyn_cons)
            for k in range(32):
                dma("sync", (lambda ti, k: lambda e: e.dma_start(
                    out=yn_t[:, k, :],
                    in_=self.yn_d[k * 128:(k + 1) * 128, ti * WT:(ti + 1) * WT]))(ti, k),
                    LYN, SYN)
            for mg in range(4):
                p.wait("gpsimd", SP, fwo)
                for k in range(32):
                    dma("gpsimd", (lambda mg, k: lambda e: e.dma_start(
                        out=wo[:, k, :],
                        in_=self.woutT[k * 128:(k + 1) * 128,
                                       mg * 512:(mg + 1) * 512]))(mg, k), LWO)
                for k in range(32):
                    for m in range(4):
                        if k == 0:
                            self.bwait("tensor", bank_free.get(m))
                        t((lambda k, m: lambda e: e.matmul(
                            pb[m][:, 0:512], wo[:, k, m * 128:(m + 1) * 128],
                            yn_t[:, k, :], start=(k == 0), stop=(k == 31)))(k, m),
                          *( [LWO, LYN] if k == 0 else [] ))
                for m in range(4):
                    v((lambda mg, m: lambda e: e.tensor_copy(
                        attn[:, mg * 4 + m, :], pb[m][:, 0:512]))(mg, m), SP)
                    bank_free[m] = (SV, SV.n)
                fwo = SP.n
            lyn_cons = SP.n
            SL1 = S("sl1")
            LRW = S("lrw")
            dma("sync", (lambda ti: lambda e: e.dma_start(
                out=self.rowbuf[0:1, 0:512],
                in_=self.rowm_d[0:1, ti * WT:(ti + 1) * WT]))(ti), LRW, SL1, SA, SV)
            dma("sync", (lambda ti: lambda e: e.dma_start(
                out=self.rowbuf[0:1, 3 * WT:3 * WT + 512],
                in_=self.rowr_d[0:1, ti * WT:(ti + 1) * WT]))(ti), LRW)
            v(lambda e: e.scalar_tensor_tensor(
                self.rowbuf[0:1, 5 * WT:5 * WT + 512], self.rowbuf[0:1, 0:512], -1.0,
                self.rowbuf[0:1, 3 * WT:3 * WT + 512], ALU.mult, ALU.mult), LRW)
            t(lambda e: e.matmul(pb[4][:, 0:512], self.ones_row[:],
                                 self.rowbuf[0:1, 3 * WT:3 * WT + 512],
                                 start=True, stop=True), SV, SA)
            t(lambda e: e.matmul(pb[5][:, 0:512], self.ones_row[:],
                                 self.rowbuf[0:1, 5 * WT:5 * WT + 512],
                                 start=True, stop=True))
            for k in range(16):
                dma("sync", (lambda ti, k: lambda e: e.dma_start(
                    out=xr[k % 2][:],
                    in_=self.x_fm[k * 128:(k + 1) * 128,
                                  3 + ti * WT:3 + (ti + 1) * WT]))(ti, k), LX2, SV)
                v((lambda k: lambda e: e.tensor_mul(self.lnt[:, 0:512], xr[k % 2][:],
                                                    pb[4][:, 0:512]))(k), self.SP, LX2)
                v((lambda k: lambda e: e.tensor_add(self.lnt[:, 0:512], self.lnt[:, 0:512],
                                                    pb[5][:, 0:512]))(k))
                v((lambda k: lambda e: e.tensor_add(attn[:, k, :], attn[:, k, :],
                                                    self.lnt[:, 0:512]))(k))
            self.ln_stats(512, 16, (lambda ti: lambda k: attn[:, k, :])(ti), "ln")
            for k in range(16):
                dma("sync", (lambda ti, k: lambda e: e.dma_start(
                    out=xr[k % 2][:],
                    in_=self.x_fm[k * 128:(k + 1) * 128,
                                  3 + ti * WT:3 + (ti + 1) * WT]))(ti, k), LX2, SV)
                v((lambda k: lambda e: e.tensor_mul(self.lnt[:, 0:512], attn[:, k, :],
                                                    pb[4][:, 0:512]))(k), self.SP)
                v((lambda k: lambda e: e.tensor_add(self.lnt[:, 0:512], self.lnt[:, 0:512],
                                                    pb[5][:, 0:512]))(k))
                v((lambda k: lambda e: e.tensor_add(attn[:, k, :], self.lnt[:, 0:512],
                                                    xr[k % 2][:]))(k), LX2)
                dma("sync", (lambda ti, k: lambda e: e.dma_start(
                    out=self.x2_d[k * 128:(k + 1) * 128, ti * WT:(ti + 1) * WT],
                    in_=attn[:, k, :]))(ti, k), SX2, SV)
            self.ln_stats(512, 16, (lambda ti: lambda k: attn[:, k, :])(ti), "ln")
            self.ln_apply(512, 16, (lambda ti: lambda k: attn[:, k, :])(ti),
                          (lambda ti: lambda k: h3[:, k, ti * WT:(ti + 1) * WT])(ti))
        self.h3_done = SV.n
        esB2.close()

    def phase_c1(self):
        p, v, a, t, g, dma, S = self.p, self.v, self.a, self.t, self.g, self.dma, self.S
        nc = self.nc
        SV, SA, SP, SG, SEV = self.SV, self.SA, self.SP, self.SG, self.SEV
        T, NT = self.T, self.NT
        pb = self.pb
        self.gate()
        esC1 = contextlib.ExitStack()
        sbC = lambda name, shape, dt=F32: esC1.enter_context(nc.sbuf_tensor(name, shape, dt))
        wg = [sbC(f"wg{i}", [128, KC, 384], BF16) for i in range(2)]
        wu = [sbC(f"wu{i}", [128, KC, 384], BF16) for i in range(2)]
        sg = [sbC(f"sg{i}", [128, 512]) for i in range(2)]
        hidst = [sbC(f"hidst{i}", [128, 512], BF16) for i in range(3)]
        LWG = [S(f"lwg{i}") for i in range(2)]
        LWU = [S(f"lwu{i}") for i in range(2)]
        wg_free = {}
        SHD = [S(f"shd{i}") for i in range(3)]
        h3 = self.h3
        bank_free = {}
        sg_cons = [0, 0]
        MGS = [3] * 21 + [1]
        for mg in range(22):
            nm = MGS[mg]
            sl = mg % 2
            if mg >= 2:
                self.bwait("gpsimd", wg_free.get(sl))
            for k in range(KC):
                mw = nm * 128
                dma("gpsimd", (lambda mg, k, sl, mw: lambda e: e.dma_start(
                    out=wg[sl][:, k, 0:mw],
                    in_=self.wgT[k * 128:(k + 1) * 128, mg * 384:mg * 384 + mw]))(mg, k, sl, mw),
                    LWG[sl])
                dma("gpsimd", (lambda mg, k, sl, mw: lambda e: e.dma_start(
                    out=wu[sl][:, k, 0:mw],
                    in_=self.wuT[k * 128:(k + 1) * 128, mg * 384:mg * 384 + mw]))(mg, k, sl, mw),
                    LWU[sl])
            for ti in range(NT):
                for k in range(KC):
                    for m in range(nm):
                        if k == 0:
                            self.bwait("tensor", bank_free.get(m))
                            self.bwait("tensor", bank_free.get(3 + m))
                            p.wait("tensor", SV, self.h3_done)
                        t((lambda k, m, ti, sl: lambda e: e.matmul(
                            pb[m][:, 0:512], wg[sl][:, k, m * 128:(m + 1) * 128],
                            h3[:, k, ti * WT:(ti + 1) * WT],
                            start=(k == 0), stop=(k == KC - 1)))(k, m, ti, sl),
                          *( [LWG[sl], LWU[sl]] if k == 0 else [] ))
                        t((lambda k, m, ti, sl: lambda e: e.matmul(
                            pb[3 + m][:, 0:512], wu[sl][:, k, m * 128:(m + 1) * 128],
                            h3[:, k, ti * WT:(ti + 1) * WT],
                            start=(k == 0), stop=(k == KC - 1)))(k, m, ti, sl))
                for m in range(nm):
                    mi = mg * 3 + m
                    ssl = m % 2
                    p.wait("scalar", SV, sg_cons[ssl])
                    a((lambda m, ssl: lambda e: e.activation(
                        sg[ssl][:], pb[m][:, 0:512], AF.Silu))(m, ssl), SP)
                    hsl = (ti * 3 + m) % 3
                    p.wait("vector", SHD[hsl], SHD[hsl].n)
                    v((lambda m, ssl, hsl: lambda e: e.tensor_tensor(
                        hidst[hsl][:], sg[ssl][:], pb[3 + m][:, 0:512],
                        ALU.mult))(m, ssl, hsl), SA)
                    sg_cons[ssl] = SV.n
                    bank_free[m] = (SA, SA.n)
                    bank_free[3 + m] = (SV, SV.n)
                    dma("sync", (lambda mi, ti, hsl: lambda e: e.dma_start(
                        out=self.hid_d[mi * 128:(mi + 1) * 128, ti * WT:(ti + 1) * WT],
                        in_=hidst[hsl][:]))(mi, ti, hsl), SHD[hsl], SV)
                if ti == NT - 1:
                    wg_free[sl] = (SP, SP.n)
        esC1.close()
        self.esH3.close()

    def phase_c2(self):
        p, v, a, t, g, dma, S = self.p, self.v, self.a, self.t, self.g, self.dma, self.S
        nc = self.nc
        SV, SA, SP, SG, SEV = self.SV, self.SA, self.SP, self.SG, self.SEV
        T, NT = self.T, self.NT
        pb = self.pb
        self.gate()
        esC2 = contextlib.ExitStack()
        sbC = lambda name, shape, dt=F32: esC2.enter_context(nc.sbuf_tensor(name, shape, dt))
        wd = sbC("wd", [128, 64, 512], BF16)
        hidr = [sbC(f"hidr{i}", [128, 512], BF16) for i in range(8)]
        dstg = [sbC(f"dstg{i}", [128, 512]) for i in range(2)]
        LWD = S("lwd")
        LHD = [S(f"lhd{i}") for i in range(8)]
        SHD = [S(f"shd{i}") for i in range(3)]
        SDN = [S(f"sdn{i}") for i in range(2)]
        bank_free = {}
        hcons = {}
        fwd = 0
        for mgrp in range(4):
            p.wait("gpsimd", SP, fwd)
            for k in range(64):
                dma("gpsimd", (lambda mgrp, k: lambda e: e.dma_start(
                    out=wd[:, k, :],
                    in_=self.wdT[k * 128:(k + 1) * 128,
                                 mgrp * 512:(mgrp + 1) * 512]))(mgrp, k), LWD)
            for ti in range(NT):
                for k in range(64):
                    sl = k % 8
                    p.wait("sync", SP, hcons.get(sl, 0))
                    dma("sync", (lambda ti, k, sl: lambda e: e.dma_start(
                        out=hidr[sl][:],
                        in_=self.hid_d[k * 128:(k + 1) * 128, ti * WT:(ti + 1) * WT]))(ti, k, sl),
                        LHD[sl], SHD[0], SHD[1], SHD[2])
                    for m in range(4):
                        if k == 0:
                            self.bwait("tensor", bank_free.get(m))
                        t((lambda k, m, sl: lambda e: e.matmul(
                            pb[m][:, 0:512], wd[:, k, m * 128:(m + 1) * 128],
                            hidr[sl][:], start=(k == 0), stop=(k == 63)))(k, m, sl),
                          LHD[sl], *( [LWD] if k == 0 else [] ))
                    hcons[sl] = SP.n
                for m in range(4):
                    dsl = m % 2
                    p.wait("vector", SDN[dsl], SDN[dsl].n)
                    v((lambda m, dsl: lambda e: e.tensor_copy(
                        dstg[dsl][:], pb[m][:, 0:512]))(m, dsl), SP)
                    bank_free[m] = (SV, SV.n)
                    dma("sync", (lambda mgrp, m, ti, dsl: lambda e: e.dma_start(
                        out=self.down_d[(mgrp * 4 + m) * 128:(mgrp * 4 + m + 1) * 128,
                                        ti * WT:(ti + 1) * WT],
                        in_=dstg[dsl][:]))(mgrp, m, ti, dsl), SDN[dsl], SV)
            fwd = SP.n
        downt = sbC("downt", [128, KC, 512])
        xr2 = [sbC(f"xr2{i}", [128, 512]) for i in range(2)]
        outst = [sbC(f"outst{i}", [128, 512]) for i in range(2)]
        LDN, LX3 = S("ldn"), S("lx3")
        STO = [S(f"sto{i}") for i in range(2)]
        SX2 = S("sx2")
        dt_cons = 0
        xr_cons = [0, 0]
        for ti in range(NT):
            p.wait("sync", SV, dt_cons)
            for k in range(KC):
                dma("sync", (lambda ti, k: lambda e: e.dma_start(
                    out=downt[:, k, :],
                    in_=self.down_d[k * 128:(k + 1) * 128, ti * WT:(ti + 1) * WT]))(ti, k),
                    LDN, SDN[0], SDN[1])
            self.ln_stats(512, KC, (lambda ti: lambda k: downt[:, k, :])(ti), "ln",
                          src_deps=(LDN,))
            for k in range(KC):
                p.wait("sync", SV, xr_cons[k % 2])
                dma("sync", (lambda ti, k: lambda e: e.dma_start(
                    out=xr2[k % 2][:],
                    in_=self.x2_d[k * 128:(k + 1) * 128, ti * WT:(ti + 1) * WT]))(ti, k),
                    LX3, SX2)
                v((lambda k: lambda e: e.tensor_mul(self.lnt[:, 0:512], downt[:, k, :],
                                                    pb[4][:, 0:512]))(k), self.SP)
                v((lambda k: lambda e: e.tensor_add(self.lnt[:, 0:512], self.lnt[:, 0:512],
                                                    pb[5][:, 0:512]))(k))
                p.wait("vector", STO[k % 2], STO[k % 2].n)
                v((lambda k: lambda e: e.tensor_add(outst[k % 2][:], self.lnt[:, 0:512],
                                                    xr2[k % 2][:]))(k), LX3)
                xr_cons[k % 2] = SV.n
                dma("sync", (lambda ti, k: lambda e: e.dma_start(
                    out=self.out_d[k * 128:(k + 1) * 128, ti * WT:(ti + 1) * WT],
                    in_=outst[k % 2][:]))(ti, k), STO[k % 2], SV)
            dt_cons = SV.n
        p.wait("sync", STO[0], STO[0].n)
        p.wait("sync", STO[1], STO[1].n)
        esC2.close()

    def finalize(self):
        with self.nc.Block() as block:
            self.p.emit(block)
        self.es.close()
        return self.nc


def build(T=2048, debug=False):
    b = Builder(T=T, debug=debug)
    b.phase_consts()
    b.phase_a()
    b.phase_a2()
    b.phase_b()
    b.phase_b2()
    b.phase_c1()
    b.phase_c2()
    nc = b.finalize()
    return nc


def host_inputs(inputs, T=2048, n_cores=8):
    x = np.asarray(inputs["L0_attn_x"], np.float32)
    B, L, _ = x.shape
    npc = n_cores // B
    shared = {}
    shared["winT"] = np.ascontiguousarray(np.asarray(inputs["L0_attn_W_in"]).T)
    shared["woutT"] = np.ascontiguousarray(np.asarray(inputs["L0_attn_W_out"]).T)
    shared["wgT"] = np.ascontiguousarray(np.asarray(inputs["L0_ffn_w_gate"]).T)
    shared["wuT"] = np.ascontiguousarray(np.asarray(inputs["L0_ffn_w_up"]).T)
    shared["wdT"] = np.ascontiguousarray(np.asarray(inputs["L0_ffn_w_down"]).T)
    shared["rms_col"] = np.ascontiguousarray(
        np.asarray(inputs["L0_attn_rms_w"]).reshape(KC, 128).T)
    shared["norm_col"] = np.ascontiguousarray(
        np.asarray(inputs["L0_attn_norm_w"]).reshape(ZC, 128).T)
    shared["conv_w"] = np.ascontiguousarray(
        np.asarray(inputs["L0_attn_conv_w"]).reshape(XCC, 128, 4).transpose(1, 0, 2))
    shared["conv_b"] = np.ascontiguousarray(
        np.asarray(inputs["L0_attn_conv_b"]).reshape(XCC, 128).T)
    shared["dt_bias"] = np.asarray(inputs["L0_attn_dt_bias"]).reshape(64, 1)
    shared["a_log"] = np.asarray(inputs["L0_attn_A_log"]).reshape(64, 1)
    shared["d_rep"] = np.ascontiguousarray(
        np.repeat(np.asarray(inputs["L0_attn_D"]).reshape(ZC, 2).T, 64, axis=0))
    shared["idm"] = np.eye(128, dtype=np.float32)
    shared["ones16k"] = np.ones((1, 2048), np.float32)
    trl = np.zeros((128, 512), np.float32)
    for q in range(2):
        for i in range(128):
            trl[i, q * 256 + q * 128 + i:(q + 1) * 256] = 1.0
    shared["tril"] = trl.astype(bf16)
    xt_all = [np.ascontiguousarray(x[b].T) for b in range(B)]
    in_maps = []
    for core in range(n_cores):
        b, r = core // npc, core % npc
        m = dict(shared)
        sl = xt_all[b][:, r * T:(r + 1) * T]
        if r == 0:
            halo = np.zeros((DM, 3), np.float32)
        else:
            halo = xt_all[b][:, r * T - 3:r * T]
        m["x_fm"] = np.ascontiguousarray(np.concatenate([halo, sl], axis=1))
        mvv = np.zeros((128, 3), np.float32)
        muu = np.zeros((1, 3), np.float32)
        for j in range(3):
            if j < r:
                mvv[:, j] = 1.0
            if r > j:
                muu[0, j] = 1.0
        m["mask_v"] = mvv
        m["mask_u"] = muu
        m["mask_u1m"] = 1.0 - muu
        in_maps.append(m)
    return in_maps


def kernel(**inputs):
    from concourse.bass_utils import run_bass_kernel_spmd
    T = 2048
    n_cores = 8
    nc = build(T=T)
    in_maps = host_inputs(inputs, T=T, n_cores=n_cores)
    res = run_bass_kernel_spmd(nc, in_maps, core_ids=list(range(n_cores)))
    x = np.asarray(inputs["L0_attn_x"])
    B, L, _ = x.shape
    npc = n_cores // B
    out = np.empty((B, L, DM), np.float32)
    for core in range(n_cores):
        b, r = core // npc, core % npc
        out[b, r * T:(r + 1) * T, :] = res.results[core]["out_d"].T
    return out
